# version 10
# baseline (speedup 1.0000x reference)
import numpy as np
import ml_dtypes
from contextlib import ExitStack

import concourse.bass as bass
import concourse.mybir as mybir
from concourse.bass_utils import run_bass_kernel_spmd

F32 = mybir.dt.float32
BF16 = mybir.dt.bfloat16
U32 = mybir.dt.uint32
AF = mybir.ActivationFunctionType
OP = mybir.AluOpType
AX = mybir.AxisListType

NCORES = 8
SEQ = 2048
NSEQ = 2
TOK = SEQ * NSEQ
D = 1024
NH = 16
HD = 64
NF = 64
E = 32
H = 512
HS = 1024
CAP = 16384.0
NT = TOK // 128
GT = 512
NG = SEQ // GT
NQ = 4
QT = NT // NQ
LN_EPS = 1e-5
LOG8 = float(np.log(8.0))


class Sem:
    def __init__(self, h):
        self.h = h
        self.n = 0


def build_nc(debug_keys=(), n_experts=E, trunc=None):
    nc = bass.Bass("TRN2", target_bir_lowering=False)

    x_in = nc.dram_tensor("x", [NSEQ, SEQ, D], F32, kind="ExternalInput").ap()
    xT_in = nc.dram_tensor("xT", [NSEQ, NG, 128, 8, GT], F32, kind="ExternalInput").ap()
    wkv_d = nc.dram_tensor("wkvr", [128, 8, 2 * D], F32, kind="ExternalInput").ap()
    wq_d2 = nc.dram_tensor("wqr", [128, 8, D], F32, kind="ExternalInput").ap()
    wout_d2 = nc.dram_tensor("woutr", [128, 8, D], F32, kind="ExternalInput").ap()
    bout = nc.dram_tensor("bout", [1, D], F32, kind="ExternalInput").ap()
    rfT_d = nc.dram_tensor("rfT", [NF, HD], F32, kind="ExternalInput").ap()
    wg_d = nc.dram_tensor("wg", [128, 8, E], F32, kind="ExternalInput").ap()
    w1_d = nc.dram_tensor("w1", [E, 128, 8, H], F32, kind="ExternalInput").ap()
    w2_d = nc.dram_tensor("w2", [E, 128, 4, D], F32, kind="ExternalInput").ap()
    w3_d = nc.dram_tensor("w3", [E, 128, 8, H], F32, kind="ExternalInput").ap()
    ws1_d = nc.dram_tensor("ws1", [128, 8, HS], F32, kind="ExternalInput").ap()
    ws2_d = nc.dram_tensor("ws2", [128, 8, D], F32, kind="ExternalInput").ap()
    ws3_d = nc.dram_tensor("ws3", [128, 8, HS], F32, kind="ExternalInput").ap()
    identb = nc.dram_tensor("identb", [128, 128], BF16, kind="ExternalInput").ap()
    identf = nc.dram_tensor("identf", [128, 128], F32, kind="ExternalInput").ap()
    iota_d = nc.dram_tensor("iota32", [128, E], F32, kind="ExternalInput").ap()
    ltri_d = nc.dram_tensor("ltri", [128, 128], BF16, kind="ExternalInput").ap()
    hmask_d = nc.dram_tensor("hmask", [128, 8 * 16], BF16, kind="ExternalInput").ap()
    nmask_d = nc.dram_tensor("nmask", [16, 8 * 128], BF16, kind="ExternalInput").ap()
    pmask_d = nc.dram_tensor("pmask", [16, 8 * 128], BF16, kind="ExternalInput").ap()
    capoff_d = nc.dram_tensor("capoff", [1, E], F32, kind="ExternalInput").ap()

    out_d = nc.dram_tensor("out", [NSEQ, SEQ, D], F32, kind="ExternalOutput").ap()
    out_flat = out_d.rearrange("s n d -> (s n) d")
    dbg = {}
    for k, shape in dict(x1=[TOK, D], wcap=[TOK, E], y=[TOK, D]).items():
        if k in debug_keys:
            dbg[k] = nc.dram_tensor("dbg_" + k, shape, F32, kind="ExternalOutput").ap()

    x1s = nc.dram_tensor("x1s", [TOK, D], F32, kind="Internal").ap()

    es = ExitStack()

    def sb(name, shape, dt):
        return es.enter_context(nc.sbuf_tensor(name, shape, dt))

    idb = sb("idb", [128, 128], BF16)
    idf = sb("idf", [128, 128], F32)
    iot = sb("iot", [128, E], F32)
    ltri = sb("ltri_s", [128, 128], BF16)
    hmask = sb("hmask_s", [128, 8, 16], BF16)
    nmask = sb("nmask_s", [16, 8, 128], BF16)
    pmask = sb("pmask_s", [16, 8, 128], BF16)
    ones1 = sb("ones1", [1, 128], BF16)
    onesc = sb("onesc", [128, 1], BF16)
    bdrf = sb("bdrf", [128, 128], BF16)
    bhl = sb("bhl", [1, 2 * D], BF16)
    capoff = sb("capoff_s", [1, E], F32)
    off_hl = sb("off_hl", [1, 2 * E], BF16)
    wg_s = sb("wg_s", [128, 8, E], BF16)
    cb30 = sb("cb30", [128, 1], F32)
    cbl8 = sb("cbl8", [128, 1], F32)
    cbeps = sb("cbeps", [128, 1], F32)
    x1_fm = sb("x1_fm", [128, 8, TOK], BF16)

    pbk = [es.enter_context(nc.psum_tensor(f"pb{i}", [128, 512], F32)) for i in range(7)]
    ptp = es.enter_context(nc.psum_tensor("ptp", [128, 2, 4, 128], BF16))

    def mksem(name):
        return Sem(es.enter_context(nc.semaphore(name)))

    semT = mksem("semT")
    semV = mksem("semV")
    semS = mksem("semS")
    semP = mksem("semP")
    semC = mksem("semC")
    semCG = mksem("semCG")
    semWA = mksem("semWA")
    semXW = [mksem("semXW0"), mksem("semXW1")]
    semW = [mksem("semW0"), mksem("semW1")]
    semW2 = [mksem("semW2a"), mksem("semW2b")]
    semRT = mksem("semRT")
    semX0 = [mksem("semX0a"), mksem("semX0b")]
    semXS = [mksem("semXSa"), mksem("semXSb"), mksem("semXSc"), mksem("semXSd")]
    semX2 = [mksem("semX2a"), mksem("semX2b")]
    semOu = [mksem("semOua"), mksem("semOub")]
    semD = mksem("semD")

    block = es.enter_context(nc.Block())
    TY = block.tensor
    VE = block.vector
    SC = block.scalar
    SY = block.sync
    GP = block.gpsimd

    def VSY(eng, instr, sem):
        instr.then_inc(sem.h, 1)
        sem.n += 1
        eng.wait_ge(sem.h, sem.n)

    with es:
      try:
        def _(s):
            s.dma_start(idf[:], identf[:, :]).then_inc(semC.h, 16)
            s.dma_start(iot[:], iota_d[:, :]).then_inc(semC.h, 16)
            s.dma_start(capoff[:], capoff_d[:, :]).then_inc(semC.h, 16)
            semC.n += 48
        SY(_)

        def _(g):
            g.dma_start(idb[:], identb[:, :]).then_inc(semCG.h, 16)
            g.dma_start(ltri[:], ltri_d[:, :]).then_inc(semCG.h, 16)
            g.dma_start(hmask[:, :, :], hmask_d[:, :]).then_inc(semCG.h, 16)
            g.dma_start(nmask[:, :, :], nmask_d[:, :]).then_inc(semCG.h, 16)
            g.dma_start(pmask[:, :, :], pmask_d[:, :]).then_inc(semCG.h, 16)
            g.dma_start(wg_s[:, :, :], wg_d[:, :, :]).then_inc(semCG.h, 16)
            g.memset(ones1[:], 1.0)
            g.memset(onesc[:], 1.0)
            g.memset(cb30[:], 1e-30)
            g.memset(cbl8[:], LOG8)
            g.memset(cbeps[:], LN_EPS)
            g.memset(bdrf[:], 0.0).then_inc(semP.h, 1)
            semP.n += 1
            g.wait_ge(semP.h, semP.n)
            g.dma_start(bdrf[0:64, 0:64], rfT_d[:, :]).then_inc(semCG.h, 16)
            g.dma_start(bdrf[64:128, 64:128], rfT_d[:, :]).then_inc(semCG.h, 16)
            semCG.n += 8 * 16
        GP(_)
        consts_done = semCG.n

        class _Stop(Exception):
            pass

        def stop_if(name):
            if trunc == name:
                import sys
                print(f"STOP {name}: T={semT.n} V={semV.n} S={semS.n} P={semP.n} "
                      f"WA={semWA.n} XW={[s.n for s in semXW]} W={[s.n for s in semW]} "
                      f"X0={[s.n for s in semX0]} XS={[s.n for s in semXS]}",
                      file=sys.stderr)
                raise _Stop

        halted = [False]
        with ExitStack() as ea:
          try:
            def sba(name, shape, dt):
                return ea.enter_context(nc.sbuf_tensor(name, shape, dt))

            w_att = sba("w_att", [128, 8, 2 * D], BF16)
            x0w = sba("x0w", [128, 8, 2, GT], BF16)
            kqat = sba("kqat", [128, 8, GT], BF16)
            e_fm = sba("e_fm", [128, 8, GT], BF16)
            kp_tm = sba("kp_tm", [128, 4, NH * HD], BF16)
            v_tm = sba("v_tm", [128, 4, NH, HD + 1], BF16)
            sq_b = sba("sq_b", [128, 8, GT], BF16)
            kvblk = sba("kvblk", [128, 8, 128], BF16)
            nlhs = sba("nlhs", [128, 8, 16], BF16)
            nrm_f = sba("nrm_f", [16, GT], F32)
            rec_f = sba("rec_f", [16, GT], F32)
            nrm_hl = sba("nrm_hl", [16, 2, GT], BF16)
            x0t = sba("x0t", [128, 2, D], F32)
            xs_t = sba("xs_t", [128, 4, D], F32)
            stat = sba("stat", [128, 16], F32)
            brow = sba("brow", [1, D], F32)
            tmpr = sba("tmpr", [1, D], F32)

            mk = dict(x0w_free=[0, 0], pk_free=[0, 0], prj_pk=[0, 0], out_pk=[0, 0],
                      ptp_free=[0, 0], py_free=0, watt_free=0, kv_start=[True],
                      sqb_free=[0, 0], efm_free=0, pb2_s=0, pb2_v=0,
                      recb_free=0, xtp_free=0, vones=0)
            mk.update({'ep4': 0, 'ep5': 0, 'x0tL0': 0, 'x0tL1': 0,
                       'xsv0': 0, 'xsv1': 0, 'xsv2': 0, 'xsv3': 0,
                       'xsl0': 0, 'xsl1': 0, 'xsl2': 0, 'xsl3': 0})

            def _(s):
                s.dma_start(brow[:], bout[:, :]).then_inc(semC.h, 16)
                semC.n += 16
            SY(_)
            brow_done = semC.n

            def _(g):
                g.wait_ge(semCG.h, consts_done)
                g.memset(v_tm[:, :, :, HD:HD + 1], 1.0)
                g.memset(kvblk[:], 0.0).then_inc(semP.h, 1)
                semP.n += 1
            GP(_)
            mk['vones'] = semP.n

            def _(v):
                v.wait_ge(semC.h, brow_done)
                VSY(v, v.tensor_copy(bhl[0:1, 0:D], brow[:]), semV)
                VSY(v, v.tensor_tensor(out=tmpr[:], in0=brow[:], in1=bhl[0:1, 0:D], op=OP.subtract), semV)
                v.tensor_copy(bhl[0:1, D:2 * D], tmpr[:])
                VSY(v, v.tensor_copy(off_hl[0:1, 0:E], capoff[:]), semV)
                VSY(v, v.tensor_tensor(out=tmpr[0:1, 0:E], in0=capoff[:], in1=off_hl[0:1, 0:E],
                                op=OP.subtract), semV)
                v.tensor_copy(off_hl[0:1, E:2 * E], tmpr[0:1, 0:E]).then_inc(semV.h, 1)
                semV.n += 1
            VE(_)

            stop_if('consts')
            for s_idx in range(NSEQ):
                def _(g, s_idx=s_idx):
                    if s_idx == 0:
                        g.wait_ge(semCG.h, consts_done)
                    else:
                        g.wait_ge(semT.h, mk['watt_free'])
                    g.dma_start(w_att[:, :, :], wkv_d[:, :, :]).then_inc(semWA.h, 16)
                    semWA.n += 16
                GP(_)
                wkv_ready = semWA.n

                for grp in range(NG):
                    gb = grp % 2

                    def _(g, s_idx=s_idx, grp=grp, gb=gb):
                        g.wait_ge(semT.h, mk['x0w_free'][gb])
                        g.dma_start(x0w[:, :, gb, :],
                                    xT_in[s_idx, grp]).then_inc(semXW[gb].h, 16)
                        semXW[gb].n += 16
                    GP(_)
                    xw_ready = semXW[gb].n

                    kfm_done = [0] * 8
                    kev_done = [0] * 8
                    ksq_done = [0] * 8
                    for c in range(8):
                        pj = pbk[c % 2]

                        def _(t, c=c, pj=pj, gb=gb, xw_ready=xw_ready):
                            if c == 0:
                                t.wait_ge(semXW[gb].h, xw_ready)
                                t.wait_ge(semWA.h, wkv_ready)
                                t.wait_ge(semS.h, mk['pb2_s'])
                                t.wait_ge(semV.h, mk['pb2_v'])
                            t.wait_ge(semV.h, mk['pk_free'][c % 2])
                            t.wait_ge(semS.h, mk['prj_pk'][c % 2])
                            for dc in range(8):
                                mm = t.matmul(pj[:], w_att[:, dc, c * 128:(c + 1) * 128],
                                         x0w[:, dc, gb, :], start=(dc == 0),
                                         stop=(dc == 7))
                            mm.then_inc(semT.h, 1)
                            semT.n += 1
                            kfm_done[c] = semT.n
                        TY(_)

                        def _(v, c=c, pj=pj):
                            v.wait_ge(semT.h, kfm_done[c])
                            v.tensor_copy(kqat[:, c, :], pj[:]).then_inc(semV.h, 1)
                            semV.n += 1
                            kev_done[c] = semV.n
                            v.wait_ge(semV.h, kev_done[c])
                            v.wait_ge(semT.h, mk['sqb_free'][c % 2])
                            v.tensor_tensor(out=sq_b[:, c, :], in0=pj[:],
                                            in1=kqat[:, c, :],
                                            op=OP.mult).then_inc(semV.h, 1)
                            semV.n += 1
                            ksq_done[c] = semV.n
                            mk['pk_free'][c % 2] = semV.n
                        VE(_)

                    stop_if('p1Ag%d_%d' % (grp, s_idx))
                    stop_if('p1Cg%d_%d' % (grp, s_idx))

                    v_ev = {}
                    for tile in range(4):
                        for oh in range(2):
                            def _(t, tile=tile, oh=oh):
                                t.wait_ge(semV.h, mk['py_free'])
                                for dc in range(8):
                                    mm = t.matmul(pbk[3][:],
                                             x0w[:, dc, gb, tile * 128:(tile + 1) * 128],
                                             w_att[:, dc, oh * 512 + D:D + (oh + 1) * 512],
                                             start=(dc == 0), stop=(dc == 7))
                                mm.then_inc(semT.h, 1)
                                semT.n += 1
                            TY(_)
                            vdn = semT.n

                            def _(v, tile=tile, oh=oh, vdn=vdn):
                                v.wait_ge(semT.h, vdn)
                                v.tensor_copy(
                                    v_tm[:, tile, oh * 8:(oh + 1) * 8, 0:HD],
                                    pbk[3][:].rearrange("p (h o) -> p h o", o=HD)
                                ).then_inc(semV.h, 1)
                                semV.n += 1
                                mk['py_free'] = semV.n
                            VE(_)
                            v_ev[tile] = semV.n

                    def _(t, kq=tuple(ksq_done)):
                        for c in range(8):
                            t.wait_ge(semV.h, kq[c])
                            mm = t.matmul(pbk[2][0:16, :], hmask[:, c, :], sq_b[:, c, :],
                                          start=(c == 0), stop=(c == 7))
                        mm.then_inc(semT.h, 1)
                        semT.n += 1
                        mk['sqb_free'][0] = semT.n
                        mk['sqb_free'][1] = semT.n
                    TY(_)
                    sumsq_done = semT.n

                    def _(s):
                        s.wait_ge(semT.h, sumsq_done)
                        VSY(s, s.activation(nrm_f[:], pbk[2][0:16, :], AF.Ln, bias=cb30[0:16, :]), semS)
                        s.activation(nrm_f[:], nrm_f[:], AF.Exp, scale=0.5,
                                     bias=cbl8[0:16, :]).then_inc(semS.h, 1)
                        semS.n += 1
                        mk['pb2_s'] = semS.n
                    SC(_)
                    nrm_done = semS.n

                    def _(v):
                        v.wait_ge(semS.h, nrm_done)
                        VSY(v, v.tensor_copy(nrm_hl[:, 0, :], nrm_f[:]), semV)
                        VSY(v, v.tensor_tensor(out=rec_f[:], in0=nrm_f[:], in1=nrm_hl[:, 0, :],
                                        op=OP.subtract), semV)
                        v.tensor_copy(nrm_hl[:, 1, :], rec_f[:]).then_inc(semV.h, 1)
                        semV.n += 1
                    VE(_)
                    nhl_done = semV.n

                    expbase = semS.n
                    prjd = [0] * 8
                    epf_done = [expbase + c + 1 for c in range(8)]
                    for c in range(8):
                        pj = pbk[c % 2]

                        def _(t, c=c, pj=pj, expbase=expbase):
                            if c < 2:
                                t.wait_ge(semV.h, nhl_done)
                                t.wait_ge(semS.h, mk['prj_pk'][c % 2])
                            else:
                                t.wait_ge(semS.h, expbase + c - 1)
                            mm = t.matmul(pj[:], bdrf[:], kqat[:, c, :], start=True, stop=False)
                            mm = t.matmul(pj[:], nmask[:, c, :], nrm_hl[:, 0, :],
                                     start=False, stop=False)
                            mm = t.matmul(pj[:], nmask[:, c, :], nrm_hl[:, 1, :],
                                     start=False, stop=True).then_inc(semT.h, 1)
                            semT.n += 1
                            prjd[c] = semT.n
                        TY(_)
                        mk['prj_pk'][c % 2] = expbase + c + 1

                    def _(s, prjd=prjd):
                        for c in range(8):
                            s.wait_ge(semT.h, prjd[c])
                            s.activation(e_fm[:, c, :], pbk[c % 2][:], AF.Exp,
                                         scale=0.125).then_inc(semS.h, 1)
                            semS.n += 1
                    SC(_)
                    assert semS.n == expbase + 8

                    stop_if('p1Bg%d_%d' % (grp, s_idx))
                    for tile in range(4):
                        def _(t, tile=tile):
                            t.wait_ge(semS.h, epf_done[7])
                            t.wait_ge(semV.h, mk['ptp_free'][0])
                            for c in range(8):
                                mm = t.transpose(ptp[:, c // 4, c % 4, :],
                                            e_fm[:, c, tile * 128:(tile + 1) * 128],
                                            idb[:])
                            mm.then_inc(semT.h, 1)
                            semT.n += 1
                        TY(_)
                        tpd = semT.n

                        def _(v, tile=tile, tpd=tpd):
                            v.wait_ge(semT.h, tpd)
                            v.tensor_copy(
                                kp_tm[:, tile, :],
                                ptp[:, :, :, :].rearrange("p a b n -> p (a b) n")
                            ).then_inc(semV.h, 1)
                            semV.n += 1
                            mk['ptp_free'][0] = semV.n
                        VE(_)
                    kp_ready = semV.n
                    stop_if('p1Dg%d_%d' % (grp, s_idx))
                    def _(t, grp=grp):
                        if mk['vones']:
                            t.wait_ge(semP.h, mk['vones'])
                            mk['vones'] = 0
                        t.wait_ge(semV.h, kp_ready)
                        t.wait_ge(semV.h, mk['py_free'])
                        for tile in range(4):
                            for c in range(8):
                                h0, h1 = 2 * c, 2 * c + 1
                                bank, cc = 4 + c // 4, c % 4
                                kvv = pbk[bank][:, 0:4 * (HD + 1)].rearrange(
                                    "p (a o) -> p a o", o=HD + 1)
                                first = mk['kv_start'][0] and tile == 0 and cc == 0
                                mm = t.matmul(kvv[0:64, cc, :],
                                         kp_tm[:, tile, h0 * HD:(h0 + 1) * HD],
                                         v_tm[:, tile, h0, :],
                                         start=first, stop=False,
                                         tile_position=(0, 0), skip_group_check=True)
                                mm = t.matmul(kvv[64:128, cc, :],
                                         kp_tm[:, tile, h1 * HD:(h1 + 1) * HD],
                                         v_tm[:, tile, h1, :],
                                         start=first, stop=False,
                                         tile_position=(0, 64), skip_group_check=True)
                        mk['kv_start'][0] = False
                        mm.then_inc(semT.h, 1)
                        semT.n += 1
                        mk['x0w_free'][grp % 2] = semT.n
                    TY(_)
                    stop_if('p1g%d_%d' % (grp, s_idx))

                stop_if('p1_%d' % s_idx)
                def _(v):
                    v.wait_ge(semT.h, semT.n)
                    v.wait_ge(semP.h, mk['vones'] or semP.n)
                    for bank in range(2):
                        kvv = pbk[4 + bank][:, 0:4 * (HD + 1)].rearrange(
                            "p (a o) -> p a o", o=HD + 1)
                        v.tensor_copy(
                            kvblk[0:64, bank * 4:(bank + 1) * 4, 0:64],
                            kvv[0:64, :, 0:HD])
                        v.tensor_copy(
                            kvblk[64:128, bank * 4:(bank + 1) * 4, 64:128],
                            kvv[64:128, :, 0:HD])
                    v.drain(fusable=True).then_inc(semV.h, 1)
                    semV.n += 1
                    mk['kv_start'][0] = True
                VE(_)
                kv_ev = semV.n

                def _(g):
                    g.wait_ge(semV.h, kv_ev)
                    g.memset(nlhs[:], 0.0).then_inc(semP.h, 1)
                    semP.n += 1
                GP(_)
                nl0 = semP.n

                def _(v):
                    v.wait_ge(semP.h, nl0)
                    for c in range(8):
                        bank, cc = 4 + c // 4, c % 4
                        kvv = pbk[bank][:, 0:4 * (HD + 1)].rearrange(
                            "p (a o) -> p a o", o=HD + 1)
                        v.tensor_copy(nlhs[0:64, c, 2 * c:2 * c + 1],
                                      kvv[0:64, cc, HD:HD + 1])
                        v.tensor_copy(nlhs[64:128, c, 2 * c + 1:2 * c + 2],
                                      kvv[64:128, cc, HD:HD + 1])
                    v.drain(fusable=True).then_inc(semV.h, 1)
                    semV.n += 1
                VE(_)
                nlhs_ready = semV.n

                def _(g):
                    g.wait_ge(semT.h, semT.n)
                    g.dma_start(w_att[:, :, 0:D], wq_d2[:, :, :]).then_inc(semWA.h, 16)
                    g.dma_start(w_att[:, :, D:2 * D], wout_d2[:, :, :]).then_inc(semWA.h, 16)
                    semWA.n += 32
                GP(_)
                wq_ready = semWA.n

                stop_if('kv_%d' % s_idx)
                for grp in range(NG):
                    gb = grp % 2
                    tile0 = s_idx * 16 + grp * 4

                    def _(g, s_idx=s_idx, grp=grp, gb=gb):
                        g.wait_ge(semT.h, mk['x0w_free'][gb])
                        g.dma_start(x0w[:, :, gb, :],
                                    xT_in[s_idx, grp]).then_inc(semXW[gb].h, 16)
                        semXW[gb].n += 16
                    GP(_)
                    xw_ready = semXW[gb].n

                    qev_done = [0] * 8
                    qsq_done = [0] * 8
                    for c in range(8):
                        pj = pbk[c % 2]

                        def _(t, c=c, pj=pj, gb=gb, xw_ready=xw_ready):
                            if c == 0:
                                t.wait_ge(semXW[gb].h, xw_ready)
                                t.wait_ge(semWA.h, wq_ready)
                                t.wait_ge(semS.h, mk['pb2_s'])
                                t.wait_ge(semV.h, mk['pb2_v'])
                            t.wait_ge(semV.h, mk['pk_free'][c % 2])
                            t.wait_ge(semS.h, mk['prj_pk'][c % 2])
                            t.wait_ge(semV.h, mk['out_pk'][c % 2])
                            for dc in range(8):
                                mm = t.matmul(pj[:], w_att[:, dc, c * 128:(c + 1) * 128],
                                         x0w[:, dc, gb, :], start=(dc == 0), stop=(dc == 7))
                            mm.then_inc(semT.h, 1)
                            semT.n += 1
                        TY(_)
                        qfm_done = semT.n

                        def _(v, c=c, pj=pj, qfm_done=qfm_done):
                            v.wait_ge(semT.h, qfm_done)
                            v.tensor_copy(kqat[:, c, :], pj[:]).then_inc(semV.h, 1)
                            semV.n += 1
                            qev_done[c] = semV.n
                            v.wait_ge(semV.h, qev_done[c])
                            v.wait_ge(semT.h, mk['sqb_free'][c % 2])
                            v.tensor_tensor(out=sq_b[:, c, :], in0=pj[:],
                                            in1=kqat[:, c, :],
                                            op=OP.mult).then_inc(semV.h, 1)
                            semV.n += 1
                            qsq_done[c] = semV.n
                            mk['pk_free'][c % 2] = semV.n
                        VE(_)

                    if grp == 0:
                        stop_if('p2q_%d' % s_idx)

                    def _(t, kq=tuple(qsq_done)):
                        for c in range(8):
                            t.wait_ge(semV.h, kq[c])
                            mm = t.matmul(pbk[2][0:16, :], hmask[:, c, :], sq_b[:, c, :],
                                          start=(c == 0), stop=(c == 7))
                        mm.then_inc(semT.h, 1)
                        semT.n += 1
                        mk['sqb_free'][0] = semT.n
                        mk['sqb_free'][1] = semT.n
                    TY(_)
                    sumsq_done = semT.n

                    def _(s):
                        s.wait_ge(semT.h, sumsq_done)
                        VSY(s, s.activation(nrm_f[:], pbk[2][0:16, :], AF.Ln, bias=cb30[0:16, :]), semS)
                        s.activation(nrm_f[:], nrm_f[:], AF.Exp, scale=0.5,
                                     bias=cbl8[0:16, :]).then_inc(semS.h, 1)
                        semS.n += 1
                        mk['pb2_s'] = semS.n
                    SC(_)
                    nrm_done = semS.n

                    def _(v):
                        v.wait_ge(semS.h, nrm_done)
                        VSY(v, v.tensor_copy(nrm_hl[:, 0, :], nrm_f[:]), semV)
                        VSY(v, v.tensor_tensor(out=rec_f[:], in0=nrm_f[:], in1=nrm_hl[:, 0, :],
                                        op=OP.subtract), semV)
                        v.tensor_copy(nrm_hl[:, 1, :], rec_f[:]).then_inc(semV.h, 1)
                        semV.n += 1
                    VE(_)
                    nhl_done = semV.n

                    expbase = semS.n
                    prjd = [0] * 8
                    epf_done = [expbase + c + 1 for c in range(8)]
                    for c in range(8):
                        pj = pbk[c % 2]

                        def _(t, c=c, pj=pj, expbase=expbase):
                            if c < 2:
                                t.wait_ge(semV.h, nhl_done)
                                t.wait_ge(semS.h, mk['prj_pk'][c % 2])
                            else:
                                t.wait_ge(semS.h, expbase + c - 1)
                            mm = t.matmul(pj[:], bdrf[:], kqat[:, c, :], start=True, stop=False)
                            mm = t.matmul(pj[:], nmask[:, c, :], nrm_hl[:, 0, :],
                                     start=False, stop=False)
                            mm = t.matmul(pj[:], nmask[:, c, :], nrm_hl[:, 1, :],
                                     start=False, stop=True).then_inc(semT.h, 1)
                            semT.n += 1
                            prjd[c] = semT.n
                        TY(_)
                        mk['prj_pk'][c % 2] = expbase + c + 1

                    def _(s, prjd=prjd):
                        for c in range(8):
                            s.wait_ge(semT.h, prjd[c])
                            s.activation(e_fm[:, c, :], pbk[c % 2][:], AF.Exp,
                                         scale=0.125).then_inc(semS.h, 1)
                            semS.n += 1
                    SC(_)
                    assert semS.n == expbase + 8
                    if grp == 0:
                        stop_if('p2e_%d' % s_idx)

                    def _(t):
                        t.wait_ge(semV.h, nlhs_ready)
                        t.wait_ge(semS.h, epf_done[7])
                        for c in range(8):
                            mm = t.matmul(pbk[2][0:16, :], nlhs[:, c, :], e_fm[:, c, :],
                                     start=(c == 0), stop=(c == 7))
                        mm.then_inc(semT.h, 1)
                        semT.n += 1
                    TY(_)
                    nra_done = semT.n

                    def _(v):
                        v.wait_ge(semT.h, nra_done)
                        VSY(v, v.tensor_scalar(out=rec_f[:], in0=pbk[2][0:16, :], scalar1=1e-6,
                                        scalar2=None, op0=OP.add), semV)
                        VSY(v, v.reciprocal(rec_f[:], rec_f[:]), semV)
                        VSY(v, v.tensor_copy(nrm_hl[:, 0, :], rec_f[:]), semV)
                        VSY(v, v.tensor_tensor(out=nrm_f[:], in0=rec_f[:], in1=nrm_hl[:, 0, :],
                                        op=OP.subtract), semV)
                        v.tensor_copy(nrm_hl[:, 1, :], nrm_f[:]).then_inc(semV.h, 1)
                        semV.n += 1
                        mk['pb2_v'] = semV.n
                    VE(_)
                    rhl_done = semV.n
                    if grp == 0:
                        stop_if('p2r_%d' % s_idx)

                    at_done = [0] * 8
                    for c in range(8):
                        pj = pbk[c % 2]

                        def _(t, c=c, pj=pj):
                            t.wait_ge(semV.h, mk['out_pk'][c % 2])
                            t.wait_ge(semV.h, rhl_done)
                            mm = t.matmul(pj[:], kvblk[:, c, :], e_fm[:, c, :],
                                     start=True, stop=True)
                            t.wait_ge(semV.h, mk['recb_free'])
                            mm = t.matmul(pbk[3][:], pmask[:, c, :], nrm_hl[:, 0, :],
                                     start=True, stop=False)
                            mm = t.matmul(pbk[3][:], pmask[:, c, :], nrm_hl[:, 1, :],
                                     start=False, stop=True).then_inc(semT.h, 1)
                            semT.n += 1
                        TY(_)
                        oc_done = semT.n

                        def _(v, c=c, pj=pj, oc_done=oc_done):
                            v.wait_ge(semT.h, oc_done)
                            VSY(v, v.tensor_copy(kqat[:, c, :], pj[:]), semV)
                            mk['out_pk'][c % 2] = semV.n
                            v.tensor_tensor(out=kqat[:, c, :], in0=kqat[:, c, :],
                                            in1=pbk[3][:], op=OP.mult).then_inc(semV.h, 1)
                            semV.n += 1
                            mk['recb_free'] = semV.n
                            at_done[c] = semV.n
                        VE(_)

                    if grp == 0:
                        stop_if('p2o_%d' % s_idx)
                    tp_defer = []
                    for tc in range(4):
                        tile = tile0 + tc
                        tb = tc
                        xb = tc % 2
                        yb_done = [0, 0]
                        for oh in range(2):
                            def _(t, tc=tc, oh=oh):
                                t.wait_ge(semV.h, at_done[7])
                                t.wait_ge(semV.h, mk['ep%d' % (4 + oh)])
                                for hc in range(8):
                                    mm = t.matmul(pbk[4 + oh][:],
                                             kqat[:, hc, tc * 128:(tc + 1) * 128],
                                             w_att[:, hc, D + oh * 512:D + (oh + 1) * 512],
                                             start=(hc == 0), stop=False)
                                mm = t.matmul(pbk[4 + oh][:], ones1[:],
                                         bhl[0:1, oh * 512:(oh + 1) * 512],
                                         start=False, stop=False)
                                mm = t.matmul(pbk[4 + oh][:], ones1[:],
                                         bhl[0:1, D + oh * 512:D + (oh + 1) * 512],
                                         start=False, stop=True).then_inc(semT.h, 1)
                                semT.n += 1
                                yb_done[oh] = semT.n
                            TY(_)

                        def _(s, s_idx=s_idx, tile=tile, xb=xb):
                            s.wait_ge(semV.h, mk['x0tL%d' % xb])
                            s.dma_start(x0t[:, xb, :],
                                        x_in[s_idx, (tile % 16) * 128:(tile % 16 + 1) * 128, :]
                                        ).then_inc(semX0[xb].h, 16)
                            semX0[xb].n += 16
                        SY(_)
                        x0_ready = semX0[xb].n

                        def _(v, tile=tile, tb=tb, xb=xb, yb_done=yb_done, x0_ready=x0_ready):
                            v.wait_ge(semX0[xb].h, x0_ready)
                            v.wait_ge(semT.h, yb_done[1])
                            v.wait_ge(semV.h, mk['xsv%d' % tb])
                            v.wait_ge(semXS[tb].h, mk['xsl%d' % tb])
                            v.tensor_tensor(out=xs_t[:, tb, 0:512], in0=pbk[4][:],
                                            in1=x0t[:, xb, 0:512], op=OP.add)
                            VSY(v, v.tensor_tensor(out=xs_t[:, tb, 512:1024], in0=pbk[5][:],
                                            in1=x0t[:, xb, 512:1024], op=OP.add), semV)
                            mk['ep4'] = semV.n + 1
                            mk['ep5'] = semV.n + 1
                            mk['x0tL%d' % xb] = semV.n + 1
                            v.tensor_reduce(out=stat[:, 0:1], in_=xs_t[:, tb, :],
                                            op=OP.add, axis=AX.X)
                            VSY(v, v.tensor_tensor(out=kp_tm[:, tb, :], in0=xs_t[:, tb, :],
                                            in1=xs_t[:, tb, :], op=OP.mult), semV)
                            v.tensor_reduce(out=stat[:, 1:2], in_=kp_tm[:, tb, :],
                                            op=OP.add, axis=AX.X)
                            VSY(v, v.tensor_scalar(out=stat[:, 2:3], in0=stat[:, 0:1],
                                            scalar1=1.0 / D, scalar2=None, op0=OP.mult), semV)
                            VSY(v, v.tensor_tensor(out=stat[:, 3:4], in0=stat[:, 2:3],
                                            in1=stat[:, 2:3], op=OP.mult), semV)
                            v.tensor_scalar(out=stat[:, 4:5], in0=stat[:, 1:2],
                                            scalar1=1.0 / D, scalar2=stat[:, 3:4],
                                            op0=OP.mult, op1=OP.subtract).then_inc(semV.h, 1)
                            semV.n += 1
                        VE(_)
                        st_done = semV.n

                        def _(s, st_done=st_done):
                            s.wait_ge(semV.h, st_done)
                            VSY(s, s.activation(stat[:, 5:6], stat[:, 4:5], AF.Ln, bias=cbeps[:, :]), semS)
                            s.activation(stat[:, 6:7], stat[:, 5:6], AF.Exp,
                                         scale=-0.5).then_inc(semS.h, 1)
                            semS.n += 1
                        SC(_)
                        rs_done = semS.n

                        def _(v, tb=tb, rs_done=rs_done):
                            v.wait_ge(semS.h, rs_done)
                            VSY(v, v.scalar_tensor_tensor(out=stat[:, 7:8], in0=stat[:, 2:3],
                                                   scalar=-1.0, in1=stat[:, 6:7],
                                                   op0=OP.mult, op1=OP.mult), semV)
                            v.tensor_scalar(out=xs_t[:, tb, :], in0=xs_t[:, tb, :],
                                            scalar1=stat[:, 6:7], scalar2=stat[:, 7:8],
                                            op0=OP.mult, op1=OP.add).then_inc(semV.h, 1)
                            semV.n += 1
                        VE(_)
                        x1_done = semV.n

                        def _(s, tile=tile, tb=tb, x1_done=x1_done):
                            s.wait_ge(semV.h, x1_done)
                            s.dma_start(x1s[tile * 128:(tile + 1) * 128, :],
                                        xs_t[:, tb, :]).then_inc(semXS[tb].h, 16)
                            semXS[tb].n += 16
                            if 'x1' in dbg:
                                s.dma_start(dbg['x1'][tile * 128:(tile + 1) * 128, :],
                                            xs_t[:, tb, :]).then_inc(semXS[tb].h, 16)
                                semXS[tb].n += 16
                            mk['xsl%d' % tb] = semXS[tb].n
                        SY(_)

                        tp_defer.append((tile, tb, x1_done))

                    for tile, tb, x1_done in tp_defer:
                        for half in range(2):
                            def _(t, tb=tb, half=half, x1_done=x1_done):
                                t.wait_ge(semV.h, x1_done)
                                t.wait_ge(semV.h, mk['xtp_free'])
                                for cc in range(4):
                                    c = half * 4 + cc
                                    mm = t.transpose(
                                        pbk[6][:].rearrange("p (a b) -> p a b", b=128)[:, cc, :],
                                        xs_t[:, tb, c * 128:(c + 1) * 128], idf[:])
                                mm.then_inc(semT.h, 1)
                                semT.n += 1
                            TY(_)
                            xtp_done = semT.n

                            def _(v, tile=tile, tb=tb, half=half, xtp_done=xtp_done):
                                v.wait_ge(semT.h, xtp_done)
                                v.tensor_copy(
                                    x1_fm[:, half * 4:(half + 1) * 4,
                                          tile * 128:(tile + 1) * 128],
                                    pbk[6][:].rearrange("p (a b) -> p a b", b=128)
                                ).then_inc(semV.h, 1)
                                semV.n += 1
                                mk['xtp_free'] = semV.n
                                if half == 1:
                                    mk['xsv%d' % tb] = semV.n
                            VE(_)

                    mk['x0w_free'][gb] = semT.n
                    if grp == 0:
                        stop_if('p2t_%d' % s_idx)
                    stop_if('p2g%d_%d' % (grp, s_idx))
                mk['watt_free'] = semT.n
            stop_if('attn')
          except _Stop:
            halted[0] = True
        att_done = dict(t=semT.n, v=semV.n, s=semS.n)
        if halted[0]:
            raise _Stop

        scor = sb("scor", [128, NT, E], F32)
        wcap = sb("wcap", [128, NT, E], F32)
        Mt = sb("Mt", [128, NT, E], BF16)
        m8 = sb("m8", [128, 8], F32)
        i8 = sb("i8", [128, 8], U32)
        i8f = sb("i8f", [128, 4], F32)
        tscr = sb("tscr", [128, E], F32)
        sume = sb("sume", [128, NT], F32)
        sume2 = sb("sume2", [128, NT], F32)
        m8max = sb("m8max", [128, NT], F32)
        eqw = sb("eqw", [128, 4, E], F32)
        eqm = sb("eqm", [128, 4, E], F32)
        cnt_row = sb("cnt_row", [1, NT * E], F32)
        cnt_sq = sb("cnt_sq", [NT, E], BF16)
        car_sq = sb("car_sq", [NT, E], F32)
        car_hi = sb("car_hi", [NT, E], BF16)
        car_lo = sb("car_lo", [NT, E], F32)
        car_lob = sb("car_lob", [NT, E], BF16)
        carh_row = sb("carh_row", [1, NT * E], BF16)
        carl_row = sb("carl_row", [1, NT * E], BF16)
        wflat = sb("wflat", [128, 24576], BF16)
        g_sb = sb("g_sb", [128, 2, 4, GT], BF16)
        s1_sb = sb("s1_sb", [128, 4, GT], BF16)
        y_sb = sb("y_sb", [128, QT, D], F32)
        x0t2 = sb("x0t2", [128, 2, D], F32)
        xs2 = sb("xs2", [128, 2, D], F32)
        stat2 = sb("stat2", [128, 16], F32)
        stat3 = sb("stat3", [128, 48], F32)

        def wv(slot, which):
            off = slot * 12288 + which * 4096
            hh = H if which < 2 else D
            return wflat[:, off:off + 4096].rearrange("p (c h) -> p c h", h=hh)

        class _SplitW:
            def __getitem__(self, idx):
                _, dc, hsl = idx
                base = (dc // 4) * 12288 + 8192 + (dc % 4) * 1024
                if isinstance(hsl, slice):
                    return wflat[:, base + hsl.start:base + hsl.stop]
                raise IndexError

        def wsv(which):
            if which == 0:
                return wflat[:, 0:8192].rearrange("p (c h) -> p c h", h=HS)
            if which == 1:
                return wflat[:, 12288:12288 + 8192].rearrange("p (c h) -> p c h", h=HS)
            return _SplitW()

        def gv(phys):
            return g_sb[:, phys // 4, phys % 4, :]

        mr = dict(pg_free=[0, 0], pr_free=[0, 0], x0t2f=[0, 0], xs2f=[0, 0], ydbg=0,
                  epi_barrier=att_done, epi_store=(0, 0), sq_free=[0, 0])

        gate_done = [0] * NT
        mxd = [0] * NT
        mw_done = [0] * NT
        gexpbase = semS.n
        for t_i in range(NT):
            pg = pbk[t_i % 2]

            def _(t, t_i=t_i, pg=pg):
                if t_i == 0:
                    t.wait_ge(semT.h, att_done['t'])
                    t.wait_ge(semV.h, att_done['v'])
                    t.wait_ge(semS.h, att_done['s'])
                    for s_ in semXS:
                        t.wait_ge(s_.h, s_.n)
                    t.wait_ge(semX0[0].h, semX0[0].n)
                    t.wait_ge(semX0[1].h, semX0[1].n)
                if t_i >= 2:
                    t.wait_ge(semS.h, gexpbase + t_i - 1)
                for dc in range(8):
                    mm = t.matmul(pg[:, 0:E], x1_fm[:, dc, t_i * 128:(t_i + 1) * 128],
                             wg_s[:, dc, :], start=(dc == 0), stop=(dc == 7))
                mm.then_inc(semT.h, 1)
                semT.n += 1
                gate_done[t_i] = semT.n
            TY(_)

            def _(v, t_i=t_i, pg=pg):
                v.wait_ge(semT.h, gate_done[t_i])
                v.tensor_reduce(out=m8max[:, t_i:t_i + 1], in_=pg[:, 0:E], op=OP.max,
                                axis=AX.X, negate=True).then_inc(semV.h, 1)
                semV.n += 1
                mxd[t_i] = semV.n
            VE(_)

        def _(s, gd=tuple(gate_done), md=tuple(mxd)):
            for t_i in range(NT):
                s.wait_ge(semT.h, gd[t_i])
                s.wait_ge(semV.h, md[t_i])
                s.activation(scor[:, t_i, :], pbk[t_i % 2][:, 0:E], AF.Exp,
                             bias=m8max[:, t_i:t_i + 1], scale=1.0,
                             accum_out=sume[:, t_i:t_i + 1]).then_inc(semS.h, 1)
                semS.n += 1
        SC(_)
        assert semS.n == gexpbase + NT
        mr['pg_free'] = [gexpbase + NT - 1, gexpbase + NT]

        for t_i in range(NT):
            def _(v, t_i=t_i):
                v.wait_ge(semS.h, gexpbase + t_i + 1)
                VSY(v, v.reciprocal(sume2[:, t_i:t_i + 1], sume[:, t_i:t_i + 1]), semV)
                VSY(v, v.max(out=m8[:], in_=scor[:, t_i, :]), semV)
                VSY(v, v.max_index(i8[:], m8[:], scor[:, t_i, :]), semV)
                v.tensor_copy(i8f[:], i8[:, 0:4])
                VSY(v, v.tensor_scalar(out=m8[:, 0:4], in0=m8[:, 0:4],
                                scalar1=sume2[:, t_i:t_i + 1], scalar2=None, op0=OP.mult), semV)
                for sl in range(4):
                    v.tensor_scalar(out=eqw[:, sl, :], in0=iot[:], scalar1=i8f[:, sl:sl + 1],
                                    scalar2=m8[:, sl:sl + 1], op0=OP.is_equal, op1=OP.mult)
                    VSY(v, v.tensor_scalar(out=eqm[:, sl, :], in0=iot[:], scalar1=i8f[:, sl:sl + 1],
                                    scalar2=None, op0=OP.is_equal), semV)
                v.tensor_reduce(out=wcap[:, t_i, :],
                                in_=eqw[:].rearrange("p a e -> p e a"),
                                op=OP.add, axis=AX.X)
                with nc.allow_low_precision(reason="0/1 indicator sums are bf16-exact"):
                    v.tensor_reduce(out=Mt[:, t_i, :],
                                    in_=eqm[:].rearrange("p a e -> p e a"),
                                    op=OP.add, axis=AX.X).then_inc(semV.h, 1)
                semV.n += 1
                mw_done[t_i] = semV.n
            VE(_)

        def _(t):
            for t_i in range(NT):
                t.wait_ge(semV.h, mw_done[t_i])
                bank = 4 + t_i // 16
                off = (t_i % 16) * E
                mm = t.matmul(pbk[bank][0:1, off:off + E], onesc[:], Mt[:, t_i, :],
                         start=(t_i % 16 == 0), stop=(t_i % 16 == 15),
                         skip_group_check=True)
            mm.then_inc(semT.h, 1)
            semT.n += 1
        TY(_)
        cnts_done = semT.n

        def _(v, cnts_done=cnts_done):
            v.wait_ge(semT.h, cnts_done)
            v.tensor_copy(cnt_row[0:1, 0:512], pbk[4][0:1, :])
            v.tensor_copy(cnt_row[0:1, 512:1024], pbk[5][0:1, :]).then_inc(semV.h, 1)
            semV.n += 1
        VE(_)
        cr_done = semV.n

        def _(g, cr_done=cr_done):
            g.wait_ge(semV.h, cr_done)
            g.dma_start(cnt_sq[:, :], cnt_row[0:1, :]).then_inc(semRT.h, 16)
            semRT.n += 16
        GP(_)
        csq_done = semRT.n

        def _(t, csq_done=csq_done):
            t.wait_ge(semRT.h, csq_done)
            mm = t.matmul(pbk[6][0:NT, 0:E], ltri[0:NT, 0:NT], cnt_sq[:, :],
                     start=True, stop=True).then_inc(semT.h, 1)
            semT.n += 1
        TY(_)
        car_done = semT.n

        def _(v, car_done=car_done):
            v.wait_ge(semT.h, car_done)
            VSY(v, v.tensor_copy(car_sq[:, :], pbk[6][0:NT, 0:E]), semV)
            VSY(v, v.tensor_copy(car_hi[:, :], car_sq[:, :]), semV)
            VSY(v, v.tensor_tensor(out=car_lo[:, :], in0=car_sq[:, :], in1=car_hi[:, :],
                            op=OP.subtract), semV)
            v.tensor_copy(car_lob[:, :], car_lo[:, :]).then_inc(semV.h, 1)
            semV.n += 1
        VE(_)
        chl_done = semV.n

        def _(g, chl_done=chl_done):
            g.wait_ge(semV.h, chl_done)
            g.dma_start(carh_row[0:1, :], car_hi[:, :]).then_inc(semRT.h, 16)
            g.dma_start(carl_row[0:1, :], car_lob[:, :]).then_inc(semRT.h, 16)
            semRT.n += 32
        GP(_)
        crow_done = semRT.n

        for t_i in range(NT):
            pr = pbk[6]

            def _(t, t_i=t_i, pr=pr):
                if t_i == 0:
                    t.wait_ge(semRT.h, crow_done)
                t.wait_ge(semV.h, mr['pr_free'][0])
                off = t_i * E
                mm = t.matmul(pr[:, 0:E], ltri[:, :], Mt[:, t_i, :], start=True, stop=False)
                mm = t.matmul(pr[:, 0:E], ones1[:], carh_row[0:1, off:off + E],
                         start=False, stop=False)
                mm = t.matmul(pr[:, 0:E], ones1[:], carl_row[0:1, off:off + E],
                         start=False, stop=False)
                mm = t.matmul(pr[:, 0:E], ones1[:], off_hl[0:1, 0:E], start=False, stop=False)
                mm = t.matmul(pr[:, 0:E], ones1[:], off_hl[0:1, E:2 * E],
                         start=False, stop=True).then_inc(semT.h, 1)
                semT.n += 1
            TY(_)
            rank_done = semT.n

            def _(v, t_i=t_i, pr=pr, rank_done=rank_done):
                VSY(v, v.wait_ge(semT.h, rank_done), semV)
                VSY(v, v.tensor_scalar(out=tscr[:], in0=pr[:, 0:E], scalar1=CAP,
                                scalar2=None, op0=OP.is_lt), semV)
                v.tensor_tensor(out=wcap[:, t_i, :], in0=wcap[:, t_i, :], in1=tscr[:],
                                op=OP.mult).then_inc(semV.h, 1)
                semV.n += 1
                mr['pr_free'][0] = semV.n
            VE(_)
        wcap_done = semV.n
        routing_scalar_done = semS.n

        if 'wcap' in dbg:
            def _(s):
                s.wait_ge(semV.h, wcap_done)
                for t_i in range(NT):
                    s.dma_start(dbg['wcap'][t_i * 128:(t_i + 1) * 128, :],
                                wcap[:, t_i, :]).then_inc(semD.h, 16)
                    semD.n += 16
            SY(_)

        stop_if('routing')
        wslot_free = [0, 0]
        wsloth_free = [0, 0]
        hb_free = {}
        yb_free = {}
        g_free = {}
        s1_free = [0, 0, 0, 0]
        first_wload = [True]
        gate_pb01_done = routing_scalar_done
        cnts_pb45_done = cr_done
        for q in range(NQ):
            for e_i in range(n_experts + 1):
                shared = (e_i == n_experts)
                slot = e_i % 2
                nhc = 8 if shared else 4

                def _(g, q=q, e_i=e_i, slot=slot, shared=shared):
                    if first_wload[0]:
                        g.wait_ge(semT.h, att_done['t'])
                        g.wait_ge(semV.h, att_done['v'])
                        g.wait_ge(semS.h, att_done['s'])
                        for s_ in semXS:
                            g.wait_ge(s_.h, s_.n)
                        g.wait_ge(semX0[0].h, semX0[0].n)
                        g.wait_ge(semX0[1].h, semX0[1].n)
                        first_wload[0] = False
                    if shared:
                        g.wait_ge(semT.h, wsloth_free[0])
                        g.dma_start(wsv(0), ws1_d[:, :, :]).then_inc(semW[0].h, 16)
                        semW[0].n += 16
                        g.wait_ge(semT.h, wslot_free[0])
                        g.dma_start(wflat[:, 8192:12288].rearrange("p (c h) -> p c h", h=D),
                                    ws2_d[:, 0:4, :]).then_inc(semW2[0].h, 16)
                        semW2[0].n += 16
                        g.wait_ge(semT.h, wsloth_free[1])
                        g.dma_start(wsv(1), ws3_d[:, :, :]).then_inc(semW[1].h, 16)
                        semW[1].n += 16
                        g.wait_ge(semT.h, wslot_free[1])
                        g.dma_start(wflat[:, 12288 + 8192:24576].rearrange("p (c h) -> p c h", h=D),
                                    ws2_d[:, 4:8, :]).then_inc(semW2[1].h, 16)
                        semW2[1].n += 16
                    else:
                        g.wait_ge(semT.h, wsloth_free[slot])
                        g.dma_start(wv(slot, 0), w1_d[e_i]).then_inc(semW[slot].h, 16)
                        g.dma_start(wv(slot, 1), w3_d[e_i]).then_inc(semW[slot].h, 16)
                        semW[slot].n += 32
                        g.wait_ge(semT.h, wslot_free[slot])
                        g.dma_start(wv(slot, 2), w2_d[e_i]).then_inc(semW2[slot].h, 16)
                        semW2[slot].n += 16
                GP(_)
                if shared:
                    w_waits = [(semW[0], semW[0].n), (semW[1], semW[1].n)]
                    w2_waits = [(semW2[0], semW2[0].n), (semW2[1], semW2[1].n)]
                else:
                    w_waits = [(semW[slot], semW[slot].n)]
                    w2_waits = [(semW2[slot], semW2[slot].n)]

                for grp2 in range(2):
                    tok0 = q * QT * 128 + grp2 * GT
                    hdone = [0] * nhc
                    g_cnt = [0] * nhc
                    for sub in range(nhc // 4):
                        hcs = range(sub * 4, sub * 4 + 4)
                        for hc in hcs:
                            bk = hc % 4

                            def _(t, hc=hc, bk=bk, tok0=tok0, shared=shared, slot=slot,
                                  grp2=grp2, w_waits=w_waits, q=q, e_i=e_i):
                                if hc == 0 and grp2 == 0:
                                    for ws_, wn_ in w_waits:
                                        t.wait_ge(ws_.h, wn_)
                                    if q == 0 and e_i == 0:
                                        t.wait_ge(semS.h, gate_pb01_done)
                                t.wait_ge(semV.h, hb_free.get(bk, 0))
                                wv1 = wsv(0) if shared else wv(slot, 0)
                                for dc in range(8):
                                    mm = t.matmul(pbk[bk][:],
                                                  wv1[:, dc, hc * 128:(hc + 1) * 128],
                                                  x1_fm[:, dc, tok0:tok0 + GT],
                                                  start=(dc == 0), stop=(dc == 7))
                                mm.then_inc(semT.h, 1)
                                semT.n += 1
                                hdone[hc] = semT.n
                            TY(_)

                        def _(s, hcs=hcs, s1w=tuple(s1_free)):
                            for hc in hcs:
                                s.wait_ge(semT.h, hdone[hc])
                                s.wait_ge(semV.h, s1w[hc % 4])
                                s.activation(s1_sb[:, hc % 4, :], pbk[hc % 4][:],
                                             AF.Silu).then_inc(semS.h, 1)
                                semS.n += 1
                        SC(_)
                        sil_cnt = semS.n

                        for hc in hcs:
                            bk = hc % 4

                            def _(t, hc=hc, bk=bk, tok0=tok0, shared=shared, slot=slot,
                                  sil_base=sil_cnt - 4):
                                t.wait_ge(semS.h, sil_base + (hc % 4) + 1)
                                wv3 = wsv(1) if shared else wv(slot, 1)
                                for dc in range(8):
                                    mm = t.matmul(pbk[bk][:],
                                                  wv3[:, dc, hc * 128:(hc + 1) * 128],
                                                  x1_fm[:, dc, tok0:tok0 + GT],
                                                  start=(dc == 0), stop=(dc == 7))
                                mm.then_inc(semT.h, 1)
                                semT.n += 1
                                hdone[hc] = semT.n
                            TY(_)

                        if grp2 == 1 and sub == nhc // 4 - 1:
                            if shared:
                                wsloth_free[0] = semT.n
                                wsloth_free[1] = semT.n
                            else:
                                wsloth_free[slot] = semT.n
                        def _(v, hcs=hcs, grp2=grp2, shared=shared):
                            for hc in hcs:
                                phys = (grp2 * 4 + hc) % 8 if not shared else hc
                                v.wait_ge(semT.h, hdone[hc])
                                v.wait_ge(semT.h, g_free.get(phys, 0))
                                v.tensor_tensor(out=gv(phys), in0=pbk[hc % 4][:],
                                                in1=s1_sb[:, hc % 4, :],
                                                op=OP.mult).then_inc(semV.h, 1)
                                semV.n += 1
                                hb_free[hc % 4] = semV.n
                                s1_free[hc % 4] = semV.n
                                g_cnt[hc] = semV.n
                        VE(_)
                    g_done = semV.n

                    for tc2 in range(4):
                        t_loc = grp2 * 4 + tc2
                        for oh in range(2):
                            yb = 4 + (tc2 * 2 + oh) % 3

                            def _(t, tc2=tc2, oh=oh, yb=yb, shared=shared, slot=slot,
                                  nhc=nhc, grp2=grp2, g_done=g_done, q=q, e_i=e_i,
                                  w2_waits=w2_waits):
                                t.wait_ge(semV.h, g_done)
                                if grp2 == 0 and tc2 == 0 and oh == 0:
                                    for ws_, wn_ in w2_waits:
                                        t.wait_ge(ws_.h, wn_)
                                if q == 0 and e_i == 0 and grp2 == 0 and tc2 == 0:
                                    t.wait_ge(semV.h, cnts_pb45_done)
                                if q == 0 and e_i == 0 and grp2 == 0 and yb == 6 and tc2 == 1:
                                    t.wait_ge(semV.h, wcap_done)
                                t.wait_ge(semV.h, yb_free.get(yb, 0))
                                wv2 = wsv(2) if shared else wv(slot, 2)
                                for hc in range(nhc):
                                    phys = (grp2 * 4 + hc) % 8 if not shared else hc
                                    mm = t.matmul(pbk[yb][:],
                                                  gv(phys)[:, tc2 * 128:(tc2 + 1) * 128],
                                                  wv2[:, hc, oh * 512:(oh + 1) * 512],
                                                  start=(hc == 0), stop=(hc == nhc - 1))
                                mm.then_inc(semT.h, 1)
                                semT.n += 1
                            TY(_)
                            y_done = semT.n
                            if oh == 1:
                                for hcx in range(nhc):
                                    physx = (grp2 * 4 + hcx) % 8 if not shared else hcx
                                    g_free[physx] = y_done

                            def _(v, q=q, t_loc=t_loc, oh=oh, yb=yb, e_i=e_i,
                                  shared=shared, y_done=y_done):
                                v.wait_ge(semT.h, y_done)
                                if e_i == 0 and t_loc == 0 and oh == 0 and q > 0:
                                    v.wait_ge(semOu[0].h, mr['epi_store'][0])
                                    v.wait_ge(semOu[1].h, mr['epi_store'][1])
                                if e_i == 0 and 'y' in dbg:
                                    v.wait_ge(semD.h, mr['ydbg'])
                                ysl = y_sb[:, t_loc, oh * 512:(oh + 1) * 512]
                                tile_g = q * QT + t_loc
                                if e_i == 0:
                                    mi = v.tensor_scalar(out=ysl, in0=pbk[yb][:],
                                                         scalar1=wcap[:, tile_g, 0:1],
                                                         scalar2=None, op0=OP.mult)
                                elif shared:
                                    mi = v.tensor_tensor(out=ysl, in0=pbk[yb][:], in1=ysl,
                                                         op=OP.add)
                                else:
                                    mi = v.scalar_tensor_tensor(out=ysl, in0=pbk[yb][:],
                                                                scalar=wcap[:, tile_g, e_i:e_i + 1],
                                                                in1=ysl, op0=OP.mult, op1=OP.add)
                                mi.then_inc(semV.h, 1)
                                semV.n += 1
                                yb_free[yb] = semV.n
                            VE(_)
                    if grp2 == 1:
                        if shared:
                            wslot_free[0] = semT.n
                            wslot_free[1] = semT.n
                        else:
                            wslot_free[slot] = semT.n

            y_all = semV.n
            if 'y' in dbg:
                def _(s, q=q, y_all=y_all):
                    s.wait_ge(semV.h, y_all)
                    for t_loc in range(QT):
                        tile = q * QT + t_loc
                        s.dma_start(dbg['y'][tile * 128:(tile + 1) * 128, :],
                                    y_sb[:, t_loc, :]).then_inc(semD.h, 16)
                        semD.n += 16
                    mr['ydbg'] = semD.n
                SY(_)

            add_dones = []
            for t_loc in range(QT):
                tile = q * QT + t_loc
                tb = t_loc % 2

                def _(s, tile=tile, tb=tb, q=q, t_loc=t_loc):
                    if q == 0 and t_loc < 2:
                        t_ = mr['epi_barrier']
                        s.wait_ge(semT.h, t_['t'])
                        s.wait_ge(semV.h, t_['v'])
                        s.wait_ge(semS.h, t_['s'])
                        for s_ in semXS:
                            s.wait_ge(s_.h, s_.n)
                    s.wait_ge(semV.h, mr['x0t2f'][tb])
                    s.dma_start(x0t2[:, tb, :],
                                x1s[tile * 128:(tile + 1) * 128, :]).then_inc(semX2[tb].h, 16)
                    semX2[tb].n += 16
                SY(_)
                x1_ready = semX2[tb].n

                def _(v, t_loc=t_loc, tb=tb, x1_ready=x1_ready):
                    v.wait_ge(semX2[tb].h, x1_ready)
                    if t_loc == 0 and 'y' in dbg:
                        v.wait_ge(semD.h, mr['ydbg'])
                    v.tensor_tensor(out=y_sb[:, t_loc, :], in0=y_sb[:, t_loc, :],
                                    in1=x0t2[:, tb, :], op=OP.add).then_inc(semV.h, 1)
                    semV.n += 1
                    mr['x0t2f'][tb] = semV.n
                VE(_)
                add_dones.append(semV.n)

            def _(s, add_dones=tuple(add_dones)):
                for t_loc in range(QT):
                    tb = t_loc % 2
                    s.wait_ge(semV.h, add_dones[t_loc])
                    s.wait_ge(semS.h, mr['sq_free'][tb])
                    s.activation(xs2[:, tb, :], y_sb[:, t_loc, :], AF.Square,
                                 accum_out=stat2[:, 8 + t_loc:9 + t_loc]).then_inc(semS.h, 1)
                    semS.n += 1
                    mr['sq_free'][tb] = semS.n
            SC(_)
            sq_all = semS.n

            adds_all = semV.n

            def _(v, sq_all=sq_all, adds_all=adds_all):
                v.wait_ge(semV.h, adds_all)
                VSY(v, v.tensor_reduce(out=stat2[:, 0:8], in_=y_sb[:, :, :], op=OP.add,
                                axis=AX.X), semV)
                VSY(v, v.tensor_scalar(out=stat3[:, 0:8], in0=stat2[:, 0:8],
                                scalar1=1.0 / D, scalar2=None, op0=OP.mult), semV)
                VSY(v, v.tensor_tensor(out=stat3[:, 8:16], in0=stat3[:, 0:8],
                                in1=stat3[:, 0:8], op=OP.mult), semV)
                v.wait_ge(semS.h, sq_all)
                VSY(v, v.tensor_scalar(out=stat3[:, 16:24], in0=stat2[:, 8:16],
                                scalar1=1.0 / D, scalar2=None, op0=OP.mult), semV)
                v.tensor_tensor(out=stat3[:, 16:24], in0=stat3[:, 16:24],
                                in1=stat3[:, 8:16], op=OP.subtract).then_inc(semV.h, 1)
                semV.n += 1
            VE(_)
            var_done = semV.n

            def _(s, var_done=var_done):
                s.wait_ge(semV.h, var_done)
                VSY(s, s.activation(stat3[:, 32:40], stat3[:, 16:24], AF.Ln,
                                    bias=cbeps[:, :]), semS)
                s.activation(stat3[:, 24:32], stat3[:, 32:40], AF.Exp,
                             scale=-0.5).then_inc(semS.h, 1)
                semS.n += 1
            SC(_)
            rstd_done = semS.n

            def _(v, rstd_done=rstd_done):
                v.wait_ge(semS.h, rstd_done)
                VSY(v, v.tensor_tensor(out=stat3[:, 8:16], in0=stat3[:, 0:8],
                                in1=stat3[:, 24:32], op=OP.mult), semV)
                v.tensor_scalar(out=stat3[:, 8:16], in0=stat3[:, 8:16],
                                scalar1=-1.0, scalar2=None,
                                op0=OP.mult).then_inc(semV.h, 1)
                semV.n += 1
            VE(_)
            shift_done = semV.n

            for t_loc in range(QT):
                tile = q * QT + t_loc
                tb = t_loc % 2

                def _(v, t_loc=t_loc, shift_done=shift_done):
                    if t_loc == 0:
                        v.wait_ge(semV.h, shift_done)
                    v.tensor_scalar(out=y_sb[:, t_loc, :], in0=y_sb[:, t_loc, :],
                                    scalar1=stat3[:, 24 + t_loc:25 + t_loc],
                                    scalar2=stat3[:, 8 + t_loc:9 + t_loc],
                                    op0=OP.mult, op1=OP.add).then_inc(semV.h, 1)
                    semV.n += 1
                VE(_)
                x2_done = semV.n

                def _(s, tile=tile, tb=tb, x2_done=x2_done, t_loc=t_loc):
                    s.wait_ge(semV.h, x2_done)
                    s.dma_start(out_flat[tile * 128:(tile + 1) * 128, :],
                                y_sb[:, t_loc, :]).then_inc(semOu[tb].h, 16)
                    semOu[tb].n += 16
                    mr['xs2f'][tb] = semOu[tb].n
                SY(_)
            mr['epi_store'] = (semOu[0].n, semOu[1].n)

        def _(s):
            s.wait_ge(semOu[0].h, semOu[0].n)
            s.wait_ge(semOu[1].h, semOu[1].n)
            if semD.n:
                s.wait_ge(semD.h, semD.n)
            for s_ in semXS:
                if s_.n:
                    s.wait_ge(s_.h, s_.n)
        SY(_)

        def _(g):
            g.wait_ge(semOu[0].h, semOu[0].n)
            g.wait_ge(semOu[1].h, semOu[1].n)
        GP(_)
      except _Stop:
        pass

    return nc


_CACHE = {}


def _build_cached(debug_keys=(), n_experts=E):
    key = (tuple(sorted(debug_keys)), n_experts)
    if key not in _CACHE:
        _CACHE[key] = build_nc(debug_keys, n_experts)
    return _CACHE[key]


def make_in_maps(inputs):
    bf = ml_dtypes.bfloat16
    def chunked(a, nc_, inner):
        return np.ascontiguousarray(a.reshape(nc_, 128, inner).transpose(1, 0, 2))

    x = np.ascontiguousarray(np.asarray(inputs["L0_attn_x"], np.float32))
    wqkvT = np.ascontiguousarray(np.asarray(inputs["L0_attn_w_qkv"], np.float32).T)
    wkvr = chunked(wqkvT[:, D:3 * D].copy(), 8, 2 * D)
    wqr = chunked(wqkvT[:, 0:D].copy(), 8, D)
    woutr = chunked(np.ascontiguousarray(np.asarray(inputs["L0_attn_w_out"], np.float32).T), 8, D)
    bout = np.asarray(inputs["L0_attn_b_out"], np.float32).reshape(1, D)
    rfT = np.ascontiguousarray(np.asarray(inputs["L0_attn_random_features"], np.float32).T)
    wg = chunked(np.ascontiguousarray(np.asarray(inputs["L0_ffn_Wg"], np.float32)), 8, E)
    w1 = np.stack([chunked(np.asarray(inputs["L0_ffn_W1"], np.float32)[e], 8, H) for e in range(E)])
    w2 = np.stack([chunked(np.asarray(inputs["L0_ffn_W2"], np.float32)[e], 4, D) for e in range(E)])
    w3 = np.stack([chunked(np.asarray(inputs["L0_ffn_W3"], np.float32)[e], 8, H) for e in range(E)])
    ws1 = chunked(np.ascontiguousarray(np.asarray(inputs["L0_ffn_Ws1"], np.float32)), 8, HS)
    ws2 = chunked(np.ascontiguousarray(np.asarray(inputs["L0_ffn_Ws2"], np.float32)), 8, D)
    ws3 = chunked(np.ascontiguousarray(np.asarray(inputs["L0_ffn_Ws3"], np.float32)), 8, HS)

    identf = np.eye(128, dtype=np.float32)
    identb = identf.astype(bf)
    iota = np.tile(np.arange(E, dtype=np.float32), (128, 1))
    ltri = np.triu(np.ones((128, 128), np.float32), 1).astype(bf)
    hmask = np.zeros((128, 8, 16), np.float32)
    nmask = np.zeros((16, 8, 128), np.float32)
    for c in range(8):
        hmask[0:64, c, 2 * c] = 1.0
        hmask[64:128, c, 2 * c + 1] = 1.0
        nmask[2 * c, c, 0:64] = 1.0
        nmask[2 * c + 1, c, 64:128] = 1.0
    pmask = nmask.reshape(16, 8 * 128).astype(bf)
    nmaskb = (-nmask).reshape(16, 8 * 128).astype(bf)
    hmaskb = hmask.reshape(128, 8 * 16).astype(bf)
    capoff = np.zeros((1, E), np.float32)

    in_maps = []
    for c in range(NCORES):
        xc = np.ascontiguousarray(x[2 * c:2 * c + 2])
        xT = xc.transpose(0, 2, 1)
        xTg = np.ascontiguousarray(
            xT.reshape(NSEQ, 8, 128, NG, GT).transpose(0, 3, 2, 1, 4))
        in_maps.append(dict(
            x=xc, xT=xTg, wkvr=wkvr, wqr=wqr, woutr=woutr, bout=bout, rfT=rfT,
            wg=wg, w1=w1, w2=w2, w3=w3, ws1=ws1, ws2=ws2, ws3=ws3,
            identb=identb, identf=identf, iota32=iota, ltri=ltri,
            hmask=hmaskb, nmask=nmaskb, pmask=pmask, capoff=capoff,
        ))
    return in_maps


def kernel(**inputs):
    nc = _build_cached()
    in_maps = make_in_maps(inputs)
    res = run_bass_kernel_spmd(nc, in_maps, core_ids=list(range(NCORES)))
    out = np.concatenate([np.asarray(r["out"]).reshape(NSEQ, SEQ, D)
                          for r in res.results], axis=0)
    return np.ascontiguousarray(out.astype(np.float32))


# version 13
# speedup vs baseline: 1.0490x; 1.0490x over previous
import numpy as np
import ml_dtypes
from contextlib import ExitStack

import concourse.bass as bass
import concourse.mybir as mybir
from concourse.bass_utils import run_bass_kernel_spmd

F32 = mybir.dt.float32
BF16 = mybir.dt.bfloat16
U32 = mybir.dt.uint32
AF = mybir.ActivationFunctionType
OP = mybir.AluOpType
AX = mybir.AxisListType

NCORES = 8
SEQ = 2048
NSEQ = 2
TOK = SEQ * NSEQ
D = 1024
NH = 16
HD = 64
NF = 64
E = 32
H = 512
HS = 1024
CAP = 16384.0
NT = TOK // 128
GT = 512
NG = SEQ // GT
NQ = 4
QT = NT // NQ
LN_EPS = 1e-5
LOG8 = float(np.log(8.0))


class Sem:
    def __init__(self, h):
        self.h = h
        self.n = 0


def build_nc(debug_keys=(), n_experts=E, trunc=None):
    nc = bass.Bass("TRN2", target_bir_lowering=False)

    x_in = nc.dram_tensor("x", [NSEQ, SEQ, D], F32, kind="ExternalInput").ap()
    xT_in = nc.dram_tensor("xT", [NSEQ, NG, 128, 8, GT], F32, kind="ExternalInput").ap()
    wkv_d = nc.dram_tensor("wkvr", [128, 8, 2 * D], F32, kind="ExternalInput").ap()
    wq_d2 = nc.dram_tensor("wqr", [128, 8, D], F32, kind="ExternalInput").ap()
    wout_d2 = nc.dram_tensor("woutr", [128, 8, D], F32, kind="ExternalInput").ap()
    bout = nc.dram_tensor("bout", [1, D], F32, kind="ExternalInput").ap()
    rfT_d = nc.dram_tensor("rfT", [NF, HD], F32, kind="ExternalInput").ap()
    wg_d = nc.dram_tensor("wg", [128, 8, E], F32, kind="ExternalInput").ap()
    w1_d = nc.dram_tensor("w1", [E, 128, 8, H], F32, kind="ExternalInput").ap()
    w2_d = nc.dram_tensor("w2", [E, 128, 4, D], F32, kind="ExternalInput").ap()
    w3_d = nc.dram_tensor("w3", [E, 128, 8, H], F32, kind="ExternalInput").ap()
    ws1_d = nc.dram_tensor("ws1", [128, 8, HS], F32, kind="ExternalInput").ap()
    ws2_d = nc.dram_tensor("ws2", [128, 8, D], F32, kind="ExternalInput").ap()
    ws3_d = nc.dram_tensor("ws3", [128, 8, HS], F32, kind="ExternalInput").ap()
    identb = nc.dram_tensor("identb", [128, 128], BF16, kind="ExternalInput").ap()
    identf = nc.dram_tensor("identf", [128, 128], F32, kind="ExternalInput").ap()
    iota_d = nc.dram_tensor("iota32", [128, E], F32, kind="ExternalInput").ap()
    ltri_d = nc.dram_tensor("ltri", [128, 128], BF16, kind="ExternalInput").ap()
    hmask_d = nc.dram_tensor("hmask", [128, 8 * 16], BF16, kind="ExternalInput").ap()
    nmask_d = nc.dram_tensor("nmask", [16, 8 * 128], BF16, kind="ExternalInput").ap()
    pmask_d = nc.dram_tensor("pmask", [16, 8 * 128], BF16, kind="ExternalInput").ap()
    capoff_d = nc.dram_tensor("capoff", [1, E], F32, kind="ExternalInput").ap()

    out_d = nc.dram_tensor("out", [NSEQ, SEQ, D], F32, kind="ExternalOutput").ap()
    out_flat = out_d.rearrange("s n d -> (s n) d")
    dbg = {}
    for k, shape in dict(x1=[TOK, D], wcap=[TOK, E], y=[TOK, D]).items():
        if k in debug_keys:
            dbg[k] = nc.dram_tensor("dbg_" + k, shape, F32, kind="ExternalOutput").ap()

    x1s = nc.dram_tensor("x1s", [TOK, D], F32, kind="Internal").ap()

    es = ExitStack()

    def sb(name, shape, dt):
        return es.enter_context(nc.sbuf_tensor(name, shape, dt))

    idb = sb("idb", [128, 128], BF16)
    idf = sb("idf", [128, 128], F32)
    iot = sb("iot", [128, E], F32)
    ltri = sb("ltri_s", [128, 128], BF16)
    hmask = sb("hmask_s", [128, 8, 16], BF16)
    nmask = sb("nmask_s", [16, 8, 128], BF16)
    pmask = sb("pmask_s", [16, 8, 128], BF16)
    ones1 = sb("ones1", [1, 128], BF16)
    onesc = sb("onesc", [128, 1], BF16)
    bdrf = sb("bdrf", [128, 128], BF16)
    bhl = sb("bhl", [1, 2 * D], BF16)
    capoff = sb("capoff_s", [1, E], F32)
    off_hl = sb("off_hl", [1, 2 * E], BF16)
    wg_s = sb("wg_s", [128, 8, E], BF16)
    cb30 = sb("cb30", [128, 1], F32)
    cbl8 = sb("cbl8", [128, 1], F32)
    cbeps = sb("cbeps", [128, 1], F32)
    x1_fm = sb("x1_fm", [128, 8, TOK], BF16)

    pbk = [es.enter_context(nc.psum_tensor(f"pb{i}", [128, 512], F32)) for i in range(7)]
    ptp = es.enter_context(nc.psum_tensor("ptp", [128, 2, 4, 128], BF16))

    def mksem(name):
        return Sem(es.enter_context(nc.semaphore(name)))

    semT = mksem("semT")
    semV = mksem("semV")
    semS = mksem("semS")
    semP = mksem("semP")
    semC = mksem("semC")
    semCG = mksem("semCG")
    semWA = mksem("semWA")
    semXW = [mksem("semXW0"), mksem("semXW1")]
    semW = [mksem("semW0"), mksem("semW1")]
    semW2 = [mksem("semW2a"), mksem("semW2b")]
    semRT = mksem("semRT")
    semX0 = [mksem("semX0a"), mksem("semX0b")]
    semXS = [mksem("semXSa"), mksem("semXSb"), mksem("semXSc"), mksem("semXSd")]
    semX2 = [mksem("semX2a"), mksem("semX2b")]
    semOu = [mksem("semOua"), mksem("semOub")]
    semD = mksem("semD")

    block = es.enter_context(nc.Block())
    TY = block.tensor
    VE = block.vector
    SC = block.scalar
    SY = block.sync
    GP = block.gpsimd

    def VSY(eng, instr, sem):
        instr.then_inc(sem.h, 1)
        sem.n += 1
        eng.wait_ge(sem.h, sem.n)

    with es:
      try:
        def _(s):
            s.dma_start(idf[:], identf[:, :]).then_inc(semC.h, 16)
            s.dma_start(iot[:], iota_d[:, :]).then_inc(semC.h, 16)
            s.dma_start(capoff[:], capoff_d[:, :]).then_inc(semC.h, 16)
            semC.n += 48
        SY(_)

        def _(g):
            g.dma_start(idb[:], identb[:, :]).then_inc(semCG.h, 16)
            g.dma_start(ltri[:], ltri_d[:, :]).then_inc(semCG.h, 16)
            g.dma_start(hmask[:, :, :], hmask_d[:, :]).then_inc(semCG.h, 16)
            g.dma_start(nmask[:, :, :], nmask_d[:, :]).then_inc(semCG.h, 16)
            g.dma_start(pmask[:, :, :], pmask_d[:, :]).then_inc(semCG.h, 16)
            g.dma_start(wg_s[:, :, :], wg_d[:, :, :]).then_inc(semCG.h, 16)
            g.memset(ones1[:], 1.0)
            g.memset(onesc[:], 1.0)
            g.memset(cb30[:], 1e-30)
            g.memset(cbl8[:], LOG8)
            g.memset(cbeps[:], LN_EPS)
            g.memset(bdrf[:], 0.0).then_inc(semP.h, 1)
            semP.n += 1
            g.wait_ge(semP.h, semP.n)
            g.dma_start(bdrf[0:64, 0:64], rfT_d[:, :]).then_inc(semCG.h, 16)
            g.dma_start(bdrf[64:128, 64:128], rfT_d[:, :]).then_inc(semCG.h, 16)
            semCG.n += 8 * 16
        GP(_)
        consts_done = semCG.n

        class _Stop(Exception):
            pass

        def stop_if(name):
            if trunc == name:
                import sys
                print(f"STOP {name}: T={semT.n} V={semV.n} S={semS.n} P={semP.n} "
                      f"WA={semWA.n} XW={[s.n for s in semXW]} W={[s.n for s in semW]} "
                      f"X0={[s.n for s in semX0]} XS={[s.n for s in semXS]}",
                      file=sys.stderr)
                raise _Stop

        halted = [False]
        with ExitStack() as ea:
          try:
            def sba(name, shape, dt):
                return ea.enter_context(nc.sbuf_tensor(name, shape, dt))

            w_att = sba("w_att", [128, 8, 2 * D], BF16)
            x0w = sba("x0w", [128, 8, 2, GT], BF16)
            kqat = sba("kqat", [128, 8, GT], BF16)
            e_fm = sba("e_fm", [128, 8, GT], BF16)
            kp_tm = sba("kp_tm", [128, 4, NH * HD], BF16)
            v_tm = sba("v_tm", [128, 4, NH, HD + 1], BF16)
            sq_b = sba("sq_b", [128, 8, GT], BF16)
            kvblk = sba("kvblk", [128, 8, 128], BF16)
            nlhs = sba("nlhs", [128, 8, 16], BF16)
            nrm_f = sba("nrm_f", [16, GT], F32)
            rec_f = sba("rec_f", [16, GT], F32)
            nrm_hl = sba("nrm_hl", [16, 2, GT], BF16)
            x0t = sba("x0t", [128, 2, D], F32)
            xs_t = sba("xs_t", [128, 4, D], F32)
            stat = sba("stat", [128, 16], F32)
            brow = sba("brow", [1, D], F32)
            tmpr = sba("tmpr", [1, D], F32)

            mk = dict(x0w_free=[0, 0], pk_free=[0, 0], prj_pk=[0, 0], out_pk=[0, 0],
                      ptp_free=[0, 0], py_free=0, watt_free=0, kv_start=[True],
                      sqb_free=[0, 0], efm_free=0, pb2_s=0, pb2_v=0,
                      recb_free=0, xtp_free=0, vones=0)
            mk.update({'ep4': 0, 'ep5': 0, 'x0tL0': 0, 'x0tL1': 0,
                       'xsv0': 0, 'xsv1': 0, 'xsv2': 0, 'xsv3': 0,
                       'xsl0': 0, 'xsl1': 0, 'xsl2': 0, 'xsl3': 0})

            def _(s):
                s.dma_start(brow[:], bout[:, :]).then_inc(semC.h, 16)
                semC.n += 16
            SY(_)
            brow_done = semC.n

            def _(g):
                g.wait_ge(semCG.h, consts_done)
                g.memset(v_tm[:, :, :, HD:HD + 1], 1.0)
                g.memset(kvblk[:], 0.0).then_inc(semP.h, 1)
                semP.n += 1
            GP(_)
            mk['vones'] = semP.n

            def _(v):
                v.wait_ge(semC.h, brow_done)
                VSY(v, v.tensor_copy(bhl[0:1, 0:D], brow[:]), semV)
                VSY(v, v.tensor_tensor(out=tmpr[:], in0=brow[:], in1=bhl[0:1, 0:D], op=OP.subtract), semV)
                v.tensor_copy(bhl[0:1, D:2 * D], tmpr[:])
                VSY(v, v.tensor_copy(off_hl[0:1, 0:E], capoff[:]), semV)
                VSY(v, v.tensor_tensor(out=tmpr[0:1, 0:E], in0=capoff[:], in1=off_hl[0:1, 0:E],
                                op=OP.subtract), semV)
                v.tensor_copy(off_hl[0:1, E:2 * E], tmpr[0:1, 0:E]).then_inc(semV.h, 1)
                semV.n += 1
            VE(_)

            stop_if('consts')
            for s_idx in range(NSEQ):
                def _(g, s_idx=s_idx):
                    if s_idx == 0:
                        g.wait_ge(semCG.h, consts_done)
                    else:
                        g.wait_ge(semT.h, mk['watt_free'])
                    g.dma_start(w_att[:, :, :], wkv_d[:, :, :]).then_inc(semWA.h, 16)
                    semWA.n += 16
                GP(_)
                wkv_ready = semWA.n

                for grp in range(NG):
                    gb = grp % 2

                    def _(g, s_idx=s_idx, grp=grp, gb=gb):
                        g.wait_ge(semT.h, mk['x0w_free'][gb])
                        g.dma_start(x0w[:, :, gb, :],
                                    xT_in[s_idx, grp]).then_inc(semXW[gb].h, 16)
                        semXW[gb].n += 16
                    GP(_)
                    xw_ready = semXW[gb].n

                    kfm_done = [0] * 8
                    kev_done = [0] * 8
                    ksq_done = [0] * 8
                    for c in range(8):
                        pj = pbk[c % 2]

                        def _(t, c=c, pj=pj, gb=gb, xw_ready=xw_ready):
                            if c == 0:
                                t.wait_ge(semXW[gb].h, xw_ready)
                                t.wait_ge(semWA.h, wkv_ready)
                                t.wait_ge(semS.h, mk['pb2_s'])
                                t.wait_ge(semV.h, mk['pb2_v'])
                            t.wait_ge(semV.h, mk['pk_free'][c % 2])
                            t.wait_ge(semS.h, mk['prj_pk'][c % 2])
                            for dc in range(8):
                                mm = t.matmul(pj[:], w_att[:, dc, c * 128:(c + 1) * 128],
                                         x0w[:, dc, gb, :], start=(dc == 0),
                                         stop=(dc == 7))
                            mm.then_inc(semT.h, 1)
                            semT.n += 1
                            kfm_done[c] = semT.n
                        TY(_)

                        def _(v, c=c, pj=pj):
                            v.wait_ge(semT.h, kfm_done[c])
                            v.tensor_copy(kqat[:, c, :], pj[:]).then_inc(semV.h, 1)
                            semV.n += 1
                            kev_done[c] = semV.n
                            v.wait_ge(semV.h, kev_done[c])
                            v.wait_ge(semT.h, mk['sqb_free'][c % 2])
                            v.tensor_tensor(out=sq_b[:, c, :], in0=pj[:],
                                            in1=kqat[:, c, :],
                                            op=OP.mult).then_inc(semV.h, 1)
                            semV.n += 1
                            ksq_done[c] = semV.n
                            mk['pk_free'][c % 2] = semV.n
                        VE(_)

                    stop_if('p1Ag%d_%d' % (grp, s_idx))
                    stop_if('p1Cg%d_%d' % (grp, s_idx))

                    v_ev = {}
                    for tile in range(4):
                        for oh in range(2):
                            def _(t, tile=tile, oh=oh):
                                t.wait_ge(semV.h, mk['py_free'])
                                for dc in range(8):
                                    mm = t.matmul(pbk[3][:],
                                             x0w[:, dc, gb, tile * 128:(tile + 1) * 128],
                                             w_att[:, dc, oh * 512 + D:D + (oh + 1) * 512],
                                             start=(dc == 0), stop=(dc == 7))
                                mm.then_inc(semT.h, 1)
                                semT.n += 1
                            TY(_)
                            vdn = semT.n

                            def _(v, tile=tile, oh=oh, vdn=vdn):
                                v.wait_ge(semT.h, vdn)
                                v.tensor_copy(
                                    v_tm[:, tile, oh * 8:(oh + 1) * 8, 0:HD],
                                    pbk[3][:].rearrange("p (h o) -> p h o", o=HD)
                                ).then_inc(semV.h, 1)
                                semV.n += 1
                                mk['py_free'] = semV.n
                            VE(_)
                            v_ev[tile] = semV.n

                    def _(t, kq=tuple(ksq_done)):
                        for c in range(8):
                            t.wait_ge(semV.h, kq[c])
                            mm = t.matmul(pbk[2][0:16, :], hmask[:, c, :], sq_b[:, c, :],
                                          start=(c == 0), stop=(c == 7))
                        mm.then_inc(semT.h, 1)
                        semT.n += 1
                        mk['sqb_free'][0] = semT.n
                        mk['sqb_free'][1] = semT.n
                    TY(_)
                    sumsq_done = semT.n

                    def _(s):
                        s.wait_ge(semT.h, sumsq_done)
                        VSY(s, s.activation(nrm_f[:], pbk[2][0:16, :], AF.Ln, bias=cb30[0:16, :]), semS)
                        s.activation(nrm_f[:], nrm_f[:], AF.Exp, scale=0.5,
                                     bias=cbl8[0:16, :]).then_inc(semS.h, 1)
                        semS.n += 1
                        mk['pb2_s'] = semS.n
                    SC(_)
                    nrm_done = semS.n

                    def _(v):
                        v.wait_ge(semS.h, nrm_done)
                        VSY(v, v.tensor_copy(nrm_hl[:, 0, :], nrm_f[:]), semV)
                        VSY(v, v.tensor_tensor(out=rec_f[:], in0=nrm_f[:], in1=nrm_hl[:, 0, :],
                                        op=OP.subtract), semV)
                        v.tensor_copy(nrm_hl[:, 1, :], rec_f[:]).then_inc(semV.h, 1)
                        semV.n += 1
                    VE(_)
                    nhl_done = semV.n

                    expbase = semS.n
                    prjd = [0] * 8
                    epf_done = [expbase + c + 1 for c in range(8)]
                    for c in range(8):
                        pj = pbk[c % 2]

                        def _(t, c=c, pj=pj, expbase=expbase):
                            if c < 2:
                                t.wait_ge(semV.h, nhl_done)
                                t.wait_ge(semS.h, mk['prj_pk'][c % 2])
                            else:
                                t.wait_ge(semS.h, expbase + c - 1)
                            mm = t.matmul(pj[:], bdrf[:], kqat[:, c, :], start=True, stop=False)
                            mm = t.matmul(pj[:], nmask[:, c, :], nrm_hl[:, 0, :],
                                     start=False, stop=False)
                            mm = t.matmul(pj[:], nmask[:, c, :], nrm_hl[:, 1, :],
                                     start=False, stop=True).then_inc(semT.h, 1)
                            semT.n += 1
                            prjd[c] = semT.n
                        TY(_)
                        mk['prj_pk'][c % 2] = expbase + c + 1

                    def _(s, prjd=prjd):
                        for c in range(8):
                            s.wait_ge(semT.h, prjd[c])
                            s.activation(e_fm[:, c, :], pbk[c % 2][:], AF.Exp,
                                         scale=0.125).then_inc(semS.h, 1)
                            semS.n += 1
                    SC(_)
                    assert semS.n == expbase + 8

                    stop_if('p1Bg%d_%d' % (grp, s_idx))
                    for tile in range(4):
                        def _(t, tile=tile):
                            t.wait_ge(semS.h, epf_done[7])
                            t.wait_ge(semV.h, mk['ptp_free'][0])
                            for c in range(8):
                                mm = t.transpose(ptp[:, c // 4, c % 4, :],
                                            e_fm[:, c, tile * 128:(tile + 1) * 128],
                                            idb[:])
                            mm.then_inc(semT.h, 1)
                            semT.n += 1
                        TY(_)
                        tpd = semT.n

                        def _(v, tile=tile, tpd=tpd):
                            v.wait_ge(semT.h, tpd)
                            v.tensor_copy(
                                kp_tm[:, tile, :],
                                ptp[:, :, :, :].rearrange("p a b n -> p (a b) n")
                            ).then_inc(semV.h, 1)
                            semV.n += 1
                            mk['ptp_free'][0] = semV.n
                        VE(_)
                    kp_ready = semV.n
                    stop_if('p1Dg%d_%d' % (grp, s_idx))
                    def _(t, grp=grp):
                        if mk['vones']:
                            t.wait_ge(semP.h, mk['vones'])
                            mk['vones'] = 0
                        t.wait_ge(semV.h, kp_ready)
                        t.wait_ge(semV.h, mk['py_free'])
                        for tile in range(4):
                            for c in range(8):
                                h0, h1 = 2 * c, 2 * c + 1
                                bank, cc = 4 + c // 4, c % 4
                                kvv = pbk[bank][:, 0:4 * (HD + 1)].rearrange(
                                    "p (a o) -> p a o", o=HD + 1)
                                first = mk['kv_start'][0] and tile == 0 and cc == 0
                                mm = t.matmul(kvv[0:64, cc, :],
                                         kp_tm[:, tile, h0 * HD:(h0 + 1) * HD],
                                         v_tm[:, tile, h0, :],
                                         start=first, stop=False,
                                         tile_position=(0, 0), skip_group_check=True)
                                mm = t.matmul(kvv[64:128, cc, :],
                                         kp_tm[:, tile, h1 * HD:(h1 + 1) * HD],
                                         v_tm[:, tile, h1, :],
                                         start=first, stop=False,
                                         tile_position=(0, 64), skip_group_check=True)
                        mk['kv_start'][0] = False
                        mm.then_inc(semT.h, 1)
                        semT.n += 1
                        mk['x0w_free'][grp % 2] = semT.n
                    TY(_)
                    stop_if('p1g%d_%d' % (grp, s_idx))

                stop_if('p1_%d' % s_idx)
                def _(v):
                    v.wait_ge(semT.h, semT.n)
                    v.wait_ge(semP.h, mk['vones'] or semP.n)
                    for bank in range(2):
                        kvv = pbk[4 + bank][:, 0:4 * (HD + 1)].rearrange(
                            "p (a o) -> p a o", o=HD + 1)
                        v.tensor_copy(
                            kvblk[0:64, bank * 4:(bank + 1) * 4, 0:64],
                            kvv[0:64, :, 0:HD])
                        v.tensor_copy(
                            kvblk[64:128, bank * 4:(bank + 1) * 4, 64:128],
                            kvv[64:128, :, 0:HD])
                    v.drain(fusable=True).then_inc(semV.h, 1)
                    semV.n += 1
                    mk['kv_start'][0] = True
                VE(_)
                kv_ev = semV.n

                def _(g):
                    g.wait_ge(semV.h, kv_ev)
                    g.memset(nlhs[:], 0.0).then_inc(semP.h, 1)
                    semP.n += 1
                GP(_)
                nl0 = semP.n

                def _(v):
                    v.wait_ge(semP.h, nl0)
                    for c in range(8):
                        bank, cc = 4 + c // 4, c % 4
                        kvv = pbk[bank][:, 0:4 * (HD + 1)].rearrange(
                            "p (a o) -> p a o", o=HD + 1)
                        v.tensor_copy(nlhs[0:64, c, 2 * c:2 * c + 1],
                                      kvv[0:64, cc, HD:HD + 1])
                        v.tensor_copy(nlhs[64:128, c, 2 * c + 1:2 * c + 2],
                                      kvv[64:128, cc, HD:HD + 1])
                    v.drain(fusable=True).then_inc(semV.h, 1)
                    semV.n += 1
                VE(_)
                nlhs_ready = semV.n

                def _(g):
                    g.wait_ge(semT.h, semT.n)
                    g.dma_start(w_att[:, :, 0:D], wq_d2[:, :, :]).then_inc(semWA.h, 16)
                    g.dma_start(w_att[:, :, D:2 * D], wout_d2[:, :, :]).then_inc(semWA.h, 16)
                    semWA.n += 32
                GP(_)
                wq_ready = semWA.n

                stop_if('kv_%d' % s_idx)
                for grp in range(NG):
                    gb = grp % 2
                    tile0 = s_idx * 16 + grp * 4

                    def _(g, s_idx=s_idx, grp=grp, gb=gb):
                        g.wait_ge(semT.h, mk['x0w_free'][gb])
                        g.dma_start(x0w[:, :, gb, :],
                                    xT_in[s_idx, grp]).then_inc(semXW[gb].h, 16)
                        semXW[gb].n += 16
                    GP(_)
                    xw_ready = semXW[gb].n

                    qev_done = [0] * 8
                    qsq_done = [0] * 8
                    for c in range(8):
                        pj = pbk[c % 2]

                        def _(t, c=c, pj=pj, gb=gb, xw_ready=xw_ready):
                            if c == 0:
                                t.wait_ge(semXW[gb].h, xw_ready)
                                t.wait_ge(semWA.h, wq_ready)
                                t.wait_ge(semS.h, mk['pb2_s'])
                                t.wait_ge(semV.h, mk['pb2_v'])
                            t.wait_ge(semV.h, mk['pk_free'][c % 2])
                            t.wait_ge(semS.h, mk['prj_pk'][c % 2])
                            t.wait_ge(semV.h, mk['out_pk'][c % 2])
                            for dc in range(8):
                                mm = t.matmul(pj[:], w_att[:, dc, c * 128:(c + 1) * 128],
                                         x0w[:, dc, gb, :], start=(dc == 0), stop=(dc == 7))
                            mm.then_inc(semT.h, 1)
                            semT.n += 1
                        TY(_)
                        qfm_done = semT.n

                        def _(v, c=c, pj=pj, qfm_done=qfm_done):
                            v.wait_ge(semT.h, qfm_done)
                            v.tensor_copy(kqat[:, c, :], pj[:]).then_inc(semV.h, 1)
                            semV.n += 1
                            qev_done[c] = semV.n
                            v.wait_ge(semV.h, qev_done[c])
                            v.wait_ge(semT.h, mk['sqb_free'][c % 2])
                            v.tensor_tensor(out=sq_b[:, c, :], in0=pj[:],
                                            in1=kqat[:, c, :],
                                            op=OP.mult).then_inc(semV.h, 1)
                            semV.n += 1
                            qsq_done[c] = semV.n
                            mk['pk_free'][c % 2] = semV.n
                        VE(_)

                    if grp == 0:
                        stop_if('p2q_%d' % s_idx)

                    def _(t, kq=tuple(qsq_done)):
                        for c in range(8):
                            t.wait_ge(semV.h, kq[c])
                            mm = t.matmul(pbk[2][0:16, :], hmask[:, c, :], sq_b[:, c, :],
                                          start=(c == 0), stop=(c == 7))
                        mm.then_inc(semT.h, 1)
                        semT.n += 1
                        mk['sqb_free'][0] = semT.n
                        mk['sqb_free'][1] = semT.n
                    TY(_)
                    sumsq_done = semT.n

                    def _(s):
                        s.wait_ge(semT.h, sumsq_done)
                        VSY(s, s.activation(nrm_f[:], pbk[2][0:16, :], AF.Ln, bias=cb30[0:16, :]), semS)
                        s.activation(nrm_f[:], nrm_f[:], AF.Exp, scale=0.5,
                                     bias=cbl8[0:16, :]).then_inc(semS.h, 1)
                        semS.n += 1
                        mk['pb2_s'] = semS.n
                    SC(_)
                    nrm_done = semS.n

                    def _(v):
                        v.wait_ge(semS.h, nrm_done)
                        VSY(v, v.tensor_copy(nrm_hl[:, 0, :], nrm_f[:]), semV)
                        VSY(v, v.tensor_tensor(out=rec_f[:], in0=nrm_f[:], in1=nrm_hl[:, 0, :],
                                        op=OP.subtract), semV)
                        v.tensor_copy(nrm_hl[:, 1, :], rec_f[:]).then_inc(semV.h, 1)
                        semV.n += 1
                    VE(_)
                    nhl_done = semV.n

                    expbase = semS.n
                    prjd = [0] * 8
                    epf_done = [expbase + c + 1 for c in range(8)]
                    for c in range(8):
                        pj = pbk[c % 2]

                        def _(t, c=c, pj=pj, expbase=expbase):
                            if c < 2:
                                t.wait_ge(semV.h, nhl_done)
                                t.wait_ge(semS.h, mk['prj_pk'][c % 2])
                            else:
                                t.wait_ge(semS.h, expbase + c - 1)
                            mm = t.matmul(pj[:], bdrf[:], kqat[:, c, :], start=True, stop=False)
                            mm = t.matmul(pj[:], nmask[:, c, :], nrm_hl[:, 0, :],
                                     start=False, stop=False)
                            mm = t.matmul(pj[:], nmask[:, c, :], nrm_hl[:, 1, :],
                                     start=False, stop=True).then_inc(semT.h, 1)
                            semT.n += 1
                            prjd[c] = semT.n
                        TY(_)
                        mk['prj_pk'][c % 2] = expbase + c + 1

                    def _(s, prjd=prjd):
                        for c in range(8):
                            s.wait_ge(semT.h, prjd[c])
                            s.activation(e_fm[:, c, :], pbk[c % 2][:], AF.Exp,
                                         scale=0.125).then_inc(semS.h, 1)
                            semS.n += 1
                    SC(_)
                    assert semS.n == expbase + 8
                    if grp == 0:
                        stop_if('p2e_%d' % s_idx)

                    def _(t):
                        t.wait_ge(semV.h, nlhs_ready)
                        t.wait_ge(semS.h, epf_done[7])
                        for c in range(8):
                            mm = t.matmul(pbk[2][0:16, :], nlhs[:, c, :], e_fm[:, c, :],
                                     start=(c == 0), stop=(c == 7))
                        mm.then_inc(semT.h, 1)
                        semT.n += 1
                    TY(_)
                    nra_done = semT.n

                    def _(v):
                        v.wait_ge(semT.h, nra_done)
                        VSY(v, v.tensor_scalar(out=rec_f[:], in0=pbk[2][0:16, :], scalar1=1e-6,
                                        scalar2=None, op0=OP.add), semV)
                        VSY(v, v.reciprocal(rec_f[:], rec_f[:]), semV)
                        VSY(v, v.tensor_copy(nrm_hl[:, 0, :], rec_f[:]), semV)
                        VSY(v, v.tensor_tensor(out=nrm_f[:], in0=rec_f[:], in1=nrm_hl[:, 0, :],
                                        op=OP.subtract), semV)
                        v.tensor_copy(nrm_hl[:, 1, :], nrm_f[:]).then_inc(semV.h, 1)
                        semV.n += 1
                        mk['pb2_v'] = semV.n
                    VE(_)
                    rhl_done = semV.n
                    if grp == 0:
                        stop_if('p2r_%d' % s_idx)

                    at_done = [0] * 8
                    for c in range(8):
                        pj = pbk[c % 2]

                        def _(t, c=c, pj=pj):
                            t.wait_ge(semV.h, mk['out_pk'][c % 2])
                            t.wait_ge(semV.h, rhl_done)
                            mm = t.matmul(pj[:], kvblk[:, c, :], e_fm[:, c, :],
                                     start=True, stop=True)
                            t.wait_ge(semV.h, mk['recb_free'])
                            mm = t.matmul(pbk[3][:], pmask[:, c, :], nrm_hl[:, 0, :],
                                     start=True, stop=False)
                            mm = t.matmul(pbk[3][:], pmask[:, c, :], nrm_hl[:, 1, :],
                                     start=False, stop=True).then_inc(semT.h, 1)
                            semT.n += 1
                        TY(_)
                        oc_done = semT.n

                        def _(v, c=c, pj=pj, oc_done=oc_done):
                            v.wait_ge(semT.h, oc_done)
                            VSY(v, v.tensor_copy(kqat[:, c, :], pj[:]), semV)
                            mk['out_pk'][c % 2] = semV.n
                            v.tensor_tensor(out=kqat[:, c, :], in0=kqat[:, c, :],
                                            in1=pbk[3][:], op=OP.mult).then_inc(semV.h, 1)
                            semV.n += 1
                            mk['recb_free'] = semV.n
                            at_done[c] = semV.n
                        VE(_)

                    if grp == 0:
                        stop_if('p2o_%d' % s_idx)
                    tp_defer = []
                    for tc in range(4):
                        tile = tile0 + tc
                        tb = tc
                        xb = tc % 2
                        yb_done = [0, 0]
                        for oh in range(2):
                            def _(t, tc=tc, oh=oh):
                                t.wait_ge(semV.h, at_done[7])
                                t.wait_ge(semV.h, mk['ep%d' % (4 + oh)])
                                for hc in range(8):
                                    mm = t.matmul(pbk[4 + oh][:],
                                             kqat[:, hc, tc * 128:(tc + 1) * 128],
                                             w_att[:, hc, D + oh * 512:D + (oh + 1) * 512],
                                             start=(hc == 0), stop=False)
                                mm = t.matmul(pbk[4 + oh][:], ones1[:],
                                         bhl[0:1, oh * 512:(oh + 1) * 512],
                                         start=False, stop=False)
                                mm = t.matmul(pbk[4 + oh][:], ones1[:],
                                         bhl[0:1, D + oh * 512:D + (oh + 1) * 512],
                                         start=False, stop=True).then_inc(semT.h, 1)
                                semT.n += 1
                                yb_done[oh] = semT.n
                            TY(_)

                        def _(s, s_idx=s_idx, tile=tile, xb=xb):
                            s.wait_ge(semV.h, mk['x0tL%d' % xb])
                            s.dma_start(x0t[:, xb, :],
                                        x_in[s_idx, (tile % 16) * 128:(tile % 16 + 1) * 128, :]
                                        ).then_inc(semX0[xb].h, 16)
                            semX0[xb].n += 16
                        SY(_)
                        x0_ready = semX0[xb].n

                        def _(v, tile=tile, tb=tb, xb=xb, yb_done=yb_done, x0_ready=x0_ready):
                            v.wait_ge(semX0[xb].h, x0_ready)
                            v.wait_ge(semT.h, yb_done[1])
                            v.wait_ge(semV.h, mk['xsv%d' % tb])
                            v.wait_ge(semXS[tb].h, mk['xsl%d' % tb])
                            v.tensor_tensor(out=xs_t[:, tb, 0:512], in0=pbk[4][:],
                                            in1=x0t[:, xb, 0:512], op=OP.add)
                            VSY(v, v.tensor_tensor(out=xs_t[:, tb, 512:1024], in0=pbk[5][:],
                                            in1=x0t[:, xb, 512:1024], op=OP.add), semV)
                            mk['ep4'] = semV.n + 1
                            mk['ep5'] = semV.n + 1
                            mk['x0tL%d' % xb] = semV.n + 1
                            v.tensor_reduce(out=stat[:, 0:1], in_=xs_t[:, tb, :],
                                            op=OP.add, axis=AX.X)
                            VSY(v, v.tensor_tensor(out=kp_tm[:, tb, :], in0=xs_t[:, tb, :],
                                            in1=xs_t[:, tb, :], op=OP.mult), semV)
                            v.tensor_reduce(out=stat[:, 1:2], in_=kp_tm[:, tb, :],
                                            op=OP.add, axis=AX.X)
                            VSY(v, v.tensor_scalar(out=stat[:, 2:3], in0=stat[:, 0:1],
                                            scalar1=1.0 / D, scalar2=None, op0=OP.mult), semV)
                            VSY(v, v.tensor_tensor(out=stat[:, 3:4], in0=stat[:, 2:3],
                                            in1=stat[:, 2:3], op=OP.mult), semV)
                            v.tensor_scalar(out=stat[:, 4:5], in0=stat[:, 1:2],
                                            scalar1=1.0 / D, scalar2=stat[:, 3:4],
                                            op0=OP.mult, op1=OP.subtract).then_inc(semV.h, 1)
                            semV.n += 1
                        VE(_)
                        st_done = semV.n

                        def _(s, st_done=st_done):
                            s.wait_ge(semV.h, st_done)
                            VSY(s, s.activation(stat[:, 5:6], stat[:, 4:5], AF.Ln, bias=cbeps[:, :]), semS)
                            s.activation(stat[:, 6:7], stat[:, 5:6], AF.Exp,
                                         scale=-0.5).then_inc(semS.h, 1)
                            semS.n += 1
                        SC(_)
                        rs_done = semS.n

                        def _(v, tb=tb, rs_done=rs_done):
                            v.wait_ge(semS.h, rs_done)
                            VSY(v, v.scalar_tensor_tensor(out=stat[:, 7:8], in0=stat[:, 2:3],
                                                   scalar=-1.0, in1=stat[:, 6:7],
                                                   op0=OP.mult, op1=OP.mult), semV)
                            v.tensor_scalar(out=xs_t[:, tb, :], in0=xs_t[:, tb, :],
                                            scalar1=stat[:, 6:7], scalar2=stat[:, 7:8],
                                            op0=OP.mult, op1=OP.add).then_inc(semV.h, 1)
                            semV.n += 1
                        VE(_)
                        x1_done = semV.n

                        def _(s, tile=tile, tb=tb, x1_done=x1_done):
                            s.wait_ge(semV.h, x1_done)
                            s.dma_start(x1s[tile * 128:(tile + 1) * 128, :],
                                        xs_t[:, tb, :]).then_inc(semXS[tb].h, 16)
                            semXS[tb].n += 16
                            if 'x1' in dbg:
                                s.dma_start(dbg['x1'][tile * 128:(tile + 1) * 128, :],
                                            xs_t[:, tb, :]).then_inc(semXS[tb].h, 16)
                                semXS[tb].n += 16
                            mk['xsl%d' % tb] = semXS[tb].n
                        SY(_)

                        tp_defer.append((tile, tb, x1_done))

                    for tile, tb, x1_done in tp_defer:
                        for half in range(2):
                            def _(t, tb=tb, half=half, x1_done=x1_done):
                                t.wait_ge(semV.h, x1_done)
                                t.wait_ge(semV.h, mk['xtp_free'])
                                for cc in range(4):
                                    c = half * 4 + cc
                                    mm = t.transpose(
                                        pbk[6][:].rearrange("p (a b) -> p a b", b=128)[:, cc, :],
                                        xs_t[:, tb, c * 128:(c + 1) * 128], idf[:])
                                mm.then_inc(semT.h, 1)
                                semT.n += 1
                            TY(_)
                            xtp_done = semT.n

                            def _(v, tile=tile, tb=tb, half=half, xtp_done=xtp_done):
                                v.wait_ge(semT.h, xtp_done)
                                v.tensor_copy(
                                    x1_fm[:, half * 4:(half + 1) * 4,
                                          tile * 128:(tile + 1) * 128],
                                    pbk[6][:].rearrange("p (a b) -> p a b", b=128)
                                ).then_inc(semV.h, 1)
                                semV.n += 1
                                mk['xtp_free'] = semV.n
                                if half == 1:
                                    mk['xsv%d' % tb] = semV.n
                            VE(_)

                    mk['x0w_free'][gb] = semT.n
                    if grp == 0:
                        stop_if('p2t_%d' % s_idx)
                    stop_if('p2g%d_%d' % (grp, s_idx))
                mk['watt_free'] = semT.n
            stop_if('attn')
          except _Stop:
            halted[0] = True
        att_done = dict(t=semT.n, v=semV.n, s=semS.n)
        if halted[0]:
            raise _Stop

        scor = sb("scor", [128, NT, E], F32)
        wcap = sb("wcap", [128, NT, E], F32)
        Mt = sb("Mt", [128, NT, E], BF16)
        m8 = sb("m8", [128, 8], F32)
        i8 = sb("i8", [128, 8], U32)
        i8f = sb("i8f", [128, 4], F32)
        tscr = sb("tscr", [128, E], F32)
        sume = sb("sume", [128, NT], F32)
        sume2 = sb("sume2", [128, NT], F32)
        m8max = sb("m8max", [128, NT], F32)
        eqw = sb("eqw", [128, 4, E], F32)
        eqm = sb("eqm", [128, 4, E], F32)
        cnt_row = sb("cnt_row", [1, NT * E], F32)
        cnt_sq = sb("cnt_sq", [NT, E], BF16)
        car_sq = sb("car_sq", [NT, E], F32)
        car_hi = sb("car_hi", [NT, E], BF16)
        car_lo = sb("car_lo", [NT, E], F32)
        car_lob = sb("car_lob", [NT, E], BF16)
        carh_row = sb("carh_row", [1, NT * E], BF16)
        carl_row = sb("carl_row", [1, NT * E], BF16)
        wflat = sb("wflat", [128, 24576], BF16)
        g_sb = sb("g_sb", [128, 2, 4, GT], BF16)
        s1_sb = sb("s1_sb", [128, 4, GT], BF16)
        y_sb = sb("y_sb", [128, QT, D], F32)
        x0t2 = sb("x0t2", [128, 2, D], F32)
        xs2 = sb("xs2", [128, 2, D], F32)
        stat2 = sb("stat2", [128, 16], F32)
        stat3 = sb("stat3", [128, 48], F32)

        def wv(slot, which):
            off = slot * 12288 + which * 4096
            hh = H if which < 2 else D
            return wflat[:, off:off + 4096].rearrange("p (c h) -> p c h", h=hh)

        class _SplitW:
            def __getitem__(self, idx):
                _, dc, hsl = idx
                base = (dc // 4) * 12288 + 8192 + (dc % 4) * 1024
                if isinstance(hsl, slice):
                    return wflat[:, base + hsl.start:base + hsl.stop]
                raise IndexError

        def wsv(which):
            if which == 0:
                return wflat[:, 0:8192].rearrange("p (c h) -> p c h", h=HS)
            if which == 1:
                return wflat[:, 12288:12288 + 8192].rearrange("p (c h) -> p c h", h=HS)
            return _SplitW()

        def gv(phys):
            return g_sb[:, phys // 4, phys % 4, :]

        mr = dict(pg_free=[0, 0], pr_free=[0, 0], x0t2f=[0, 0], xs2f=[0, 0], ydbg=0,
                  epi_barrier=att_done, epi_store=(0, 0), sq_free=[0, 0])

        gate_done = [0] * NT
        mxd = [0] * NT
        mw_done = [0] * NT
        gexpbase = semS.n
        for t_i in range(NT):
            pg = pbk[t_i % 2]

            def _(t, t_i=t_i, pg=pg):
                if t_i == 0:
                    t.wait_ge(semT.h, att_done['t'])
                    t.wait_ge(semV.h, att_done['v'])
                    t.wait_ge(semS.h, att_done['s'])
                    for s_ in semXS:
                        t.wait_ge(s_.h, s_.n)
                    t.wait_ge(semX0[0].h, semX0[0].n)
                    t.wait_ge(semX0[1].h, semX0[1].n)
                if t_i >= 2:
                    t.wait_ge(semS.h, gexpbase + t_i - 1)
                for dc in range(8):
                    mm = t.matmul(pg[:, 0:E], x1_fm[:, dc, t_i * 128:(t_i + 1) * 128],
                             wg_s[:, dc, :], start=(dc == 0), stop=(dc == 7))
                mm.then_inc(semT.h, 1)
                semT.n += 1
                gate_done[t_i] = semT.n
            TY(_)

            def _(v, t_i=t_i, pg=pg):
                v.wait_ge(semT.h, gate_done[t_i])
                v.tensor_reduce(out=m8max[:, t_i:t_i + 1], in_=pg[:, 0:E], op=OP.max,
                                axis=AX.X, negate=True).then_inc(semV.h, 1)
                semV.n += 1
                mxd[t_i] = semV.n
            VE(_)

        def _(s, gd=tuple(gate_done), md=tuple(mxd)):
            for t_i in range(NT):
                s.wait_ge(semT.h, gd[t_i])
                s.wait_ge(semV.h, md[t_i])
                s.activation(scor[:, t_i, :], pbk[t_i % 2][:, 0:E], AF.Exp,
                             bias=m8max[:, t_i:t_i + 1], scale=1.0,
                             accum_out=sume[:, t_i:t_i + 1]).then_inc(semS.h, 1)
                semS.n += 1
        SC(_)
        assert semS.n == gexpbase + NT
        mr['pg_free'] = [gexpbase + NT - 1, gexpbase + NT]

        for t_i in range(NT):
            def _(v, t_i=t_i):
                v.wait_ge(semS.h, gexpbase + t_i + 1)
                VSY(v, v.reciprocal(sume2[:, t_i:t_i + 1], sume[:, t_i:t_i + 1]), semV)
                VSY(v, v.max(out=m8[:], in_=scor[:, t_i, :]), semV)
                VSY(v, v.max_index(i8[:], m8[:], scor[:, t_i, :]), semV)
                v.tensor_copy(i8f[:], i8[:, 0:4])
                VSY(v, v.tensor_scalar(out=m8[:, 0:4], in0=m8[:, 0:4],
                                scalar1=sume2[:, t_i:t_i + 1], scalar2=None, op0=OP.mult), semV)
                for sl in range(4):
                    v.tensor_scalar(out=eqw[:, sl, :], in0=iot[:], scalar1=i8f[:, sl:sl + 1],
                                    scalar2=m8[:, sl:sl + 1], op0=OP.is_equal, op1=OP.mult)
                    VSY(v, v.tensor_scalar(out=eqm[:, sl, :], in0=iot[:], scalar1=i8f[:, sl:sl + 1],
                                    scalar2=None, op0=OP.is_equal), semV)
                v.tensor_reduce(out=wcap[:, t_i, :],
                                in_=eqw[:].rearrange("p a e -> p e a"),
                                op=OP.add, axis=AX.X)
                with nc.allow_low_precision(reason="0/1 indicator sums are bf16-exact"):
                    v.tensor_reduce(out=Mt[:, t_i, :],
                                    in_=eqm[:].rearrange("p a e -> p e a"),
                                    op=OP.add, axis=AX.X).then_inc(semV.h, 1)
                semV.n += 1
                mw_done[t_i] = semV.n
            VE(_)

        def _(t):
            for t_i in range(NT):
                t.wait_ge(semV.h, mw_done[t_i])
                bank = 4 + t_i // 16
                off = (t_i % 16) * E
                mm = t.matmul(pbk[bank][0:1, off:off + E], onesc[:], Mt[:, t_i, :],
                         start=(t_i % 16 == 0), stop=(t_i % 16 == 15),
                         skip_group_check=True)
            mm.then_inc(semT.h, 1)
            semT.n += 1
        TY(_)
        cnts_done = semT.n

        def _(v, cnts_done=cnts_done):
            v.wait_ge(semT.h, cnts_done)
            v.tensor_copy(cnt_row[0:1, 0:512], pbk[4][0:1, :])
            v.tensor_copy(cnt_row[0:1, 512:1024], pbk[5][0:1, :]).then_inc(semV.h, 1)
            semV.n += 1
        VE(_)
        cr_done = semV.n

        def _(g, cr_done=cr_done):
            g.wait_ge(semV.h, cr_done)
            g.dma_start(cnt_sq[:, :], cnt_row[0:1, :]).then_inc(semRT.h, 16)
            semRT.n += 16
        GP(_)
        csq_done = semRT.n

        def _(t, csq_done=csq_done):
            t.wait_ge(semRT.h, csq_done)
            mm = t.matmul(pbk[6][0:NT, 0:E], ltri[0:NT, 0:NT], cnt_sq[:, :],
                     start=True, stop=True).then_inc(semT.h, 1)
            semT.n += 1
        TY(_)
        car_done = semT.n

        def _(v, car_done=car_done):
            v.wait_ge(semT.h, car_done)
            VSY(v, v.tensor_copy(car_sq[:, :], pbk[6][0:NT, 0:E]), semV)
            VSY(v, v.tensor_copy(car_hi[:, :], car_sq[:, :]), semV)
            VSY(v, v.tensor_tensor(out=car_lo[:, :], in0=car_sq[:, :], in1=car_hi[:, :],
                            op=OP.subtract), semV)
            v.tensor_copy(car_lob[:, :], car_lo[:, :]).then_inc(semV.h, 1)
            semV.n += 1
        VE(_)
        chl_done = semV.n

        def _(g, chl_done=chl_done):
            g.wait_ge(semV.h, chl_done)
            g.dma_start(carh_row[0:1, :], car_hi[:, :]).then_inc(semRT.h, 16)
            g.dma_start(carl_row[0:1, :], car_lob[:, :]).then_inc(semRT.h, 16)
            semRT.n += 32
        GP(_)
        crow_done = semRT.n

        for t_i in range(NT):
            pr = pbk[6]

            def _(t, t_i=t_i, pr=pr):
                if t_i == 0:
                    t.wait_ge(semRT.h, crow_done)
                t.wait_ge(semV.h, mr['pr_free'][0])
                off = t_i * E
                mm = t.matmul(pr[:, 0:E], ltri[:, :], Mt[:, t_i, :], start=True, stop=False)
                mm = t.matmul(pr[:, 0:E], ones1[:], carh_row[0:1, off:off + E],
                         start=False, stop=False)
                mm = t.matmul(pr[:, 0:E], ones1[:], carl_row[0:1, off:off + E],
                         start=False, stop=False)
                mm = t.matmul(pr[:, 0:E], ones1[:], off_hl[0:1, 0:E], start=False, stop=False)
                mm = t.matmul(pr[:, 0:E], ones1[:], off_hl[0:1, E:2 * E],
                         start=False, stop=True).then_inc(semT.h, 1)
                semT.n += 1
            TY(_)
            rank_done = semT.n

            def _(v, t_i=t_i, pr=pr, rank_done=rank_done):
                VSY(v, v.wait_ge(semT.h, rank_done), semV)
                VSY(v, v.tensor_scalar(out=tscr[:], in0=pr[:, 0:E], scalar1=CAP,
                                scalar2=None, op0=OP.is_lt), semV)
                v.tensor_tensor(out=wcap[:, t_i, :], in0=wcap[:, t_i, :], in1=tscr[:],
                                op=OP.mult).then_inc(semV.h, 1)
                semV.n += 1
                mr['pr_free'][0] = semV.n
            VE(_)
        wcap_done = semV.n
        routing_scalar_done = semS.n

        if 'wcap' in dbg:
            def _(s):
                s.wait_ge(semV.h, wcap_done)
                for t_i in range(NT):
                    s.dma_start(dbg['wcap'][t_i * 128:(t_i + 1) * 128, :],
                                wcap[:, t_i, :]).then_inc(semD.h, 16)
                    semD.n += 16
            SY(_)

        stop_if('routing')
        wslot_free = [0, 0]
        wsloth_free = [0, 0]
        hb_free = {}
        yb_free = {}
        g_free = {}
        s1_free = [0, 0, 0, 0]
        first_wload = [True]
        gate_pb01_done = routing_scalar_done
        cnts_pb45_done = cr_done
        for q in range(NQ):
            for e_i in range(n_experts + 1):
                shared = (e_i == n_experts)
                slot = e_i % 2
                nhc = 8 if shared else 4

                def _(g, q=q, e_i=e_i, slot=slot, shared=shared):
                    if first_wload[0]:
                        g.wait_ge(semT.h, att_done['t'])
                        g.wait_ge(semV.h, att_done['v'])
                        g.wait_ge(semS.h, att_done['s'])
                        for s_ in semXS:
                            g.wait_ge(s_.h, s_.n)
                        g.wait_ge(semX0[0].h, semX0[0].n)
                        g.wait_ge(semX0[1].h, semX0[1].n)
                        first_wload[0] = False
                    if shared:
                        g.wait_ge(semT.h, wsloth_free[0])
                        g.dma_start(wsv(0), ws1_d[:, :, :]).then_inc(semW[0].h, 16)
                        semW[0].n += 16
                        g.wait_ge(semT.h, wslot_free[0])
                        g.dma_start(wflat[:, 8192:12288].rearrange("p (c h) -> p c h", h=D),
                                    ws2_d[:, 0:4, :]).then_inc(semW2[0].h, 16)
                        semW2[0].n += 16
                        g.wait_ge(semT.h, wsloth_free[1])
                        g.dma_start(wsv(1), ws3_d[:, :, :]).then_inc(semW[1].h, 16)
                        semW[1].n += 16
                        g.wait_ge(semT.h, wslot_free[1])
                        g.dma_start(wflat[:, 12288 + 8192:24576].rearrange("p (c h) -> p c h", h=D),
                                    ws2_d[:, 4:8, :]).then_inc(semW2[1].h, 16)
                        semW2[1].n += 16
                    else:
                        g.wait_ge(semT.h, wsloth_free[slot])
                        g.dma_start(wv(slot, 0), w1_d[e_i]).then_inc(semW[slot].h, 16)
                        g.dma_start(wv(slot, 1), w3_d[e_i]).then_inc(semW[slot].h, 16)
                        semW[slot].n += 32
                        g.wait_ge(semT.h, wslot_free[slot])
                        g.dma_start(wv(slot, 2), w2_d[e_i]).then_inc(semW2[slot].h, 16)
                        semW2[slot].n += 16
                GP(_)
                if shared:
                    w_waits = [(semW[0], semW[0].n), (semW[1], semW[1].n)]
                    w2_waits = [(semW2[0], semW2[0].n), (semW2[1], semW2[1].n)]
                else:
                    w_waits = [(semW[slot], semW[slot].n)]
                    w2_waits = [(semW2[slot], semW2[slot].n)]

                for grp2 in range(2):
                    tok0 = q * QT * 128 + grp2 * GT
                    hdone = [0] * nhc
                    g_cnt = [0] * nhc
                    for sub in range(nhc // 4):
                        hcs = range(sub * 4, sub * 4 + 4)
                        for hc in hcs:
                            bk = hc % 4

                            def _(t, hc=hc, bk=bk, tok0=tok0, shared=shared, slot=slot,
                                  grp2=grp2, w_waits=w_waits, q=q, e_i=e_i, sub=sub):
                                if hc == 0 and grp2 == 0:
                                    for ws_, wn_ in w_waits:
                                        t.wait_ge(ws_.h, wn_)
                                    if q == 0 and e_i == 0:
                                        t.wait_ge(semS.h, gate_pb01_done)
                                if hc == sub * 4:
                                    t.wait_ge(semV.h, max(hb_free.get(b, 0) for b in range(4)))
                                wv1 = wsv(0) if shared else wv(slot, 0)
                                for dc in range(8):
                                    mm = t.matmul(pbk[bk][:],
                                                  wv1[:, dc, hc * 128:(hc + 1) * 128],
                                                  x1_fm[:, dc, tok0:tok0 + GT],
                                                  start=(dc == 0), stop=(dc == 7))
                                mm.then_inc(semT.h, 1)
                                semT.n += 1
                                hdone[hc] = semT.n
                            TY(_)

                        def _(s, hcs=hcs, s1w=tuple(s1_free)):
                            for i, hc in enumerate(hcs):
                                s.wait_ge(semT.h, hdone[hc])
                                if i == 0:
                                    s.wait_ge(semV.h, max(s1w))
                                s.activation(s1_sb[:, hc % 4, :], pbk[hc % 4][:],
                                             AF.Silu).then_inc(semS.h, 1)
                                semS.n += 1
                        SC(_)
                        sil_cnt = semS.n

                        for hc in hcs:
                            bk = hc % 4

                            def _(t, hc=hc, bk=bk, tok0=tok0, shared=shared, slot=slot,
                                  sil_base=sil_cnt - 4):
                                t.wait_ge(semS.h, sil_base + (hc % 4) + 1)
                                wv3 = wsv(1) if shared else wv(slot, 1)
                                for dc in range(8):
                                    mm = t.matmul(pbk[bk][:],
                                                  wv3[:, dc, hc * 128:(hc + 1) * 128],
                                                  x1_fm[:, dc, tok0:tok0 + GT],
                                                  start=(dc == 0), stop=(dc == 7))
                                mm.then_inc(semT.h, 1)
                                semT.n += 1
                                hdone[hc] = semT.n
                            TY(_)

                        if grp2 == 1 and sub == nhc // 4 - 1:
                            if shared:
                                wsloth_free[0] = semT.n
                                wsloth_free[1] = semT.n
                            else:
                                wsloth_free[slot] = semT.n
                        def _(v, hcs=hcs, grp2=grp2, shared=shared):
                            for hc in hcs:
                                phys = (grp2 * 4 + hc) % 8 if not shared else hc
                                v.wait_ge(semT.h, max(hdone[hc], g_free.get(phys, 0)))
                                v.tensor_tensor(out=gv(phys), in0=pbk[hc % 4][:],
                                                in1=s1_sb[:, hc % 4, :],
                                                op=OP.mult).then_inc(semV.h, 1)
                                semV.n += 1
                                hb_free[hc % 4] = semV.n
                                s1_free[hc % 4] = semV.n
                                g_cnt[hc] = semV.n
                        VE(_)
                    g_done = semV.n

                    for tc2 in range(4):
                        t_loc = grp2 * 4 + tc2
                        for oh in range(2):
                            yb = 4 + (tc2 * 2 + oh) % 3

                            def _(t, tc2=tc2, oh=oh, yb=yb, shared=shared, slot=slot,
                                  nhc=nhc, grp2=grp2, g_done=g_done, q=q, e_i=e_i,
                                  w2_waits=w2_waits):
                                bidx = tc2 * 2 + oh
                                if bidx == 0:
                                    t.wait_ge(semV.h, g_done)
                                if grp2 == 0 and tc2 == 0 and oh == 0:
                                    for ws_, wn_ in w2_waits:
                                        t.wait_ge(ws_.h, wn_)
                                if q == 0 and e_i == 0 and grp2 == 0 and tc2 == 0:
                                    t.wait_ge(semV.h, cnts_pb45_done)
                                if q == 0 and e_i == 0 and grp2 == 0 and yb == 6 and tc2 == 1:
                                    t.wait_ge(semV.h, wcap_done)
                                if bidx >= 3:
                                    t.wait_ge(semV.h, yb_free.get(yb, 0))
                                wv2 = wsv(2) if shared else wv(slot, 2)
                                for hc in range(nhc):
                                    phys = (grp2 * 4 + hc) % 8 if not shared else hc
                                    mm = t.matmul(pbk[yb][:],
                                                  gv(phys)[:, tc2 * 128:(tc2 + 1) * 128],
                                                  wv2[:, hc, oh * 512:(oh + 1) * 512],
                                                  start=(hc == 0), stop=(hc == nhc - 1))
                                mm.then_inc(semT.h, 1)
                                semT.n += 1
                            TY(_)
                            y_done = semT.n
                            if oh == 1:
                                for hcx in range(nhc):
                                    physx = (grp2 * 4 + hcx) % 8 if not shared else hcx
                                    g_free[physx] = y_done

                            def _(v, q=q, t_loc=t_loc, oh=oh, yb=yb, e_i=e_i,
                                  shared=shared, y_done=y_done):
                                v.wait_ge(semT.h, y_done)
                                if e_i == 0 and t_loc == 0 and oh == 0 and q > 0:
                                    v.wait_ge(semOu[0].h, mr['epi_store'][0])
                                    v.wait_ge(semOu[1].h, mr['epi_store'][1])
                                if e_i == 0 and 'y' in dbg:
                                    v.wait_ge(semD.h, mr['ydbg'])
                                ysl = y_sb[:, t_loc, oh * 512:(oh + 1) * 512]
                                tile_g = q * QT + t_loc
                                if e_i == 0:
                                    mi = v.tensor_scalar(out=ysl, in0=pbk[yb][:],
                                                         scalar1=wcap[:, tile_g, 0:1],
                                                         scalar2=None, op0=OP.mult)
                                elif shared:
                                    mi = v.tensor_tensor(out=ysl, in0=pbk[yb][:], in1=ysl,
                                                         op=OP.add)
                                else:
                                    mi = v.scalar_tensor_tensor(out=ysl, in0=pbk[yb][:],
                                                                scalar=wcap[:, tile_g, e_i:e_i + 1],
                                                                in1=ysl, op0=OP.mult, op1=OP.add)
                                mi.then_inc(semV.h, 1)
                                semV.n += 1
                                yb_free[yb] = semV.n
                            VE(_)
                    if grp2 == 1:
                        if shared:
                            wslot_free[0] = semT.n
                            wslot_free[1] = semT.n
                        else:
                            wslot_free[slot] = semT.n

            y_all = semV.n
            if 'y' in dbg:
                def _(s, q=q, y_all=y_all):
                    s.wait_ge(semV.h, y_all)
                    for t_loc in range(QT):
                        tile = q * QT + t_loc
                        s.dma_start(dbg['y'][tile * 128:(tile + 1) * 128, :],
                                    y_sb[:, t_loc, :]).then_inc(semD.h, 16)
                        semD.n += 16
                    mr['ydbg'] = semD.n
                SY(_)

            add_dones = []
            for t_loc in range(QT):
                tile = q * QT + t_loc
                tb = t_loc % 2

                def _(s, tile=tile, tb=tb, q=q, t_loc=t_loc):
                    if q == 0 and t_loc < 2:
                        t_ = mr['epi_barrier']
                        s.wait_ge(semT.h, t_['t'])
                        s.wait_ge(semV.h, t_['v'])
                        s.wait_ge(semS.h, t_['s'])
                        for s_ in semXS:
                            s.wait_ge(s_.h, s_.n)
                    s.wait_ge(semV.h, mr['x0t2f'][tb])
                    s.dma_start(x0t2[:, tb, :],
                                x1s[tile * 128:(tile + 1) * 128, :]).then_inc(semX2[tb].h, 16)
                    semX2[tb].n += 16
                SY(_)
                x1_ready = semX2[tb].n

                def _(v, t_loc=t_loc, tb=tb, x1_ready=x1_ready):
                    v.wait_ge(semX2[tb].h, x1_ready)
                    if t_loc == 0 and 'y' in dbg:
                        v.wait_ge(semD.h, mr['ydbg'])
                    v.tensor_tensor(out=y_sb[:, t_loc, :], in0=y_sb[:, t_loc, :],
                                    in1=x0t2[:, tb, :], op=OP.add).then_inc(semV.h, 1)
                    semV.n += 1
                    mr['x0t2f'][tb] = semV.n
                VE(_)
                add_dones.append(semV.n)

            def _(s, add_dones=tuple(add_dones)):
                for t_loc in range(QT):
                    tb = t_loc % 2
                    s.wait_ge(semV.h, add_dones[t_loc])
                    s.wait_ge(semS.h, mr['sq_free'][tb])
                    s.activation(xs2[:, tb, :], y_sb[:, t_loc, :], AF.Square,
                                 accum_out=stat2[:, 8 + t_loc:9 + t_loc]).then_inc(semS.h, 1)
                    semS.n += 1
                    mr['sq_free'][tb] = semS.n
            SC(_)
            sq_all = semS.n

            adds_all = semV.n

            def _(v, sq_all=sq_all, adds_all=adds_all):
                v.wait_ge(semV.h, adds_all)
                VSY(v, v.tensor_reduce(out=stat2[:, 0:8], in_=y_sb[:, :, :], op=OP.add,
                                axis=AX.X), semV)
                VSY(v, v.tensor_scalar(out=stat3[:, 0:8], in0=stat2[:, 0:8],
                                scalar1=1.0 / D, scalar2=None, op0=OP.mult), semV)
                VSY(v, v.tensor_tensor(out=stat3[:, 8:16], in0=stat3[:, 0:8],
                                in1=stat3[:, 0:8], op=OP.mult), semV)
                v.wait_ge(semS.h, sq_all)
                VSY(v, v.tensor_scalar(out=stat3[:, 16:24], in0=stat2[:, 8:16],
                                scalar1=1.0 / D, scalar2=None, op0=OP.mult), semV)
                v.tensor_tensor(out=stat3[:, 16:24], in0=stat3[:, 16:24],
                                in1=stat3[:, 8:16], op=OP.subtract).then_inc(semV.h, 1)
                semV.n += 1
            VE(_)
            var_done = semV.n

            def _(s, var_done=var_done):
                s.wait_ge(semV.h, var_done)
                VSY(s, s.activation(stat3[:, 32:40], stat3[:, 16:24], AF.Ln,
                                    bias=cbeps[:, :]), semS)
                s.activation(stat3[:, 24:32], stat3[:, 32:40], AF.Exp,
                             scale=-0.5).then_inc(semS.h, 1)
                semS.n += 1
            SC(_)
            rstd_done = semS.n

            def _(v, rstd_done=rstd_done):
                v.wait_ge(semS.h, rstd_done)
                VSY(v, v.tensor_tensor(out=stat3[:, 8:16], in0=stat3[:, 0:8],
                                in1=stat3[:, 24:32], op=OP.mult), semV)
                v.tensor_scalar(out=stat3[:, 8:16], in0=stat3[:, 8:16],
                                scalar1=-1.0, scalar2=None,
                                op0=OP.mult).then_inc(semV.h, 1)
                semV.n += 1
            VE(_)
            shift_done = semV.n

            for t_loc in range(QT):
                tile = q * QT + t_loc
                tb = t_loc % 2

                def _(v, t_loc=t_loc, shift_done=shift_done):
                    if t_loc == 0:
                        v.wait_ge(semV.h, shift_done)
                    v.tensor_scalar(out=y_sb[:, t_loc, :], in0=y_sb[:, t_loc, :],
                                    scalar1=stat3[:, 24 + t_loc:25 + t_loc],
                                    scalar2=stat3[:, 8 + t_loc:9 + t_loc],
                                    op0=OP.mult, op1=OP.add).then_inc(semV.h, 1)
                    semV.n += 1
                VE(_)
                x2_done = semV.n

                def _(s, tile=tile, tb=tb, x2_done=x2_done, t_loc=t_loc):
                    s.wait_ge(semV.h, x2_done)
                    s.dma_start(out_flat[tile * 128:(tile + 1) * 128, :],
                                y_sb[:, t_loc, :]).then_inc(semOu[tb].h, 16)
                    semOu[tb].n += 16
                    mr['xs2f'][tb] = semOu[tb].n
                SY(_)
            mr['epi_store'] = (semOu[0].n, semOu[1].n)

        def _(s):
            s.wait_ge(semOu[0].h, semOu[0].n)
            s.wait_ge(semOu[1].h, semOu[1].n)
            if semD.n:
                s.wait_ge(semD.h, semD.n)
            for s_ in semXS:
                if s_.n:
                    s.wait_ge(s_.h, s_.n)
        SY(_)

        def _(g):
            g.wait_ge(semOu[0].h, semOu[0].n)
            g.wait_ge(semOu[1].h, semOu[1].n)
        GP(_)
      except _Stop:
        pass

    return nc


_CACHE = {}


def _build_cached(debug_keys=(), n_experts=E):
    key = (tuple(sorted(debug_keys)), n_experts)
    if key not in _CACHE:
        _CACHE[key] = build_nc(debug_keys, n_experts)
    return _CACHE[key]


def make_in_maps(inputs):
    bf = ml_dtypes.bfloat16
    def chunked(a, nc_, inner):
        return np.ascontiguousarray(a.reshape(nc_, 128, inner).transpose(1, 0, 2))

    x = np.ascontiguousarray(np.asarray(inputs["L0_attn_x"], np.float32))
    wqkvT = np.ascontiguousarray(np.asarray(inputs["L0_attn_w_qkv"], np.float32).T)
    wkvr = chunked(wqkvT[:, D:3 * D].copy(), 8, 2 * D)
    wqr = chunked(wqkvT[:, 0:D].copy(), 8, D)
    woutr = chunked(np.ascontiguousarray(np.asarray(inputs["L0_attn_w_out"], np.float32).T), 8, D)
    bout = np.asarray(inputs["L0_attn_b_out"], np.float32).reshape(1, D)
    rfT = np.ascontiguousarray(np.asarray(inputs["L0_attn_random_features"], np.float32).T)
    wg = chunked(np.ascontiguousarray(np.asarray(inputs["L0_ffn_Wg"], np.float32)), 8, E)
    w1 = np.stack([chunked(np.asarray(inputs["L0_ffn_W1"], np.float32)[e], 8, H) for e in range(E)])
    w2 = np.stack([chunked(np.asarray(inputs["L0_ffn_W2"], np.float32)[e], 4, D) for e in range(E)])
    w3 = np.stack([chunked(np.asarray(inputs["L0_ffn_W3"], np.float32)[e], 8, H) for e in range(E)])
    ws1 = chunked(np.ascontiguousarray(np.asarray(inputs["L0_ffn_Ws1"], np.float32)), 8, HS)
    ws2 = chunked(np.ascontiguousarray(np.asarray(inputs["L0_ffn_Ws2"], np.float32)), 8, D)
    ws3 = chunked(np.ascontiguousarray(np.asarray(inputs["L0_ffn_Ws3"], np.float32)), 8, HS)

    identf = np.eye(128, dtype=np.float32)
    identb = identf.astype(bf)
    iota = np.tile(np.arange(E, dtype=np.float32), (128, 1))
    ltri = np.triu(np.ones((128, 128), np.float32), 1).astype(bf)
    hmask = np.zeros((128, 8, 16), np.float32)
    nmask = np.zeros((16, 8, 128), np.float32)
    for c in range(8):
        hmask[0:64, c, 2 * c] = 1.0
        hmask[64:128, c, 2 * c + 1] = 1.0
        nmask[2 * c, c, 0:64] = 1.0
        nmask[2 * c + 1, c, 64:128] = 1.0
    pmask = nmask.reshape(16, 8 * 128).astype(bf)
    nmaskb = (-nmask).reshape(16, 8 * 128).astype(bf)
    hmaskb = hmask.reshape(128, 8 * 16).astype(bf)
    capoff = np.zeros((1, E), np.float32)

    in_maps = []
    for c in range(NCORES):
        xc = np.ascontiguousarray(x[2 * c:2 * c + 2])
        xT = xc.transpose(0, 2, 1)
        xTg = np.ascontiguousarray(
            xT.reshape(NSEQ, 8, 128, NG, GT).transpose(0, 3, 2, 1, 4))
        in_maps.append(dict(
            x=xc, xT=xTg, wkvr=wkvr, wqr=wqr, woutr=woutr, bout=bout, rfT=rfT,
            wg=wg, w1=w1, w2=w2, w3=w3, ws1=ws1, ws2=ws2, ws3=ws3,
            identb=identb, identf=identf, iota32=iota, ltri=ltri,
            hmask=hmaskb, nmask=nmaskb, pmask=pmask, capoff=capoff,
        ))
    return in_maps


def kernel(**inputs):
    nc = _build_cached()
    in_maps = make_in_maps(inputs)
    res = run_bass_kernel_spmd(nc, in_maps, core_ids=list(range(NCORES)))
    out = np.concatenate([np.asarray(r["out"]).reshape(NSEQ, SEQ, D)
                          for r in res.results], axis=0)
    return np.ascontiguousarray(out.astype(np.float32))
